# version 13
# baseline (speedup 1.0000x reference)
import numpy as np
import ml_dtypes
import concourse.bass as bass
import concourse.mybir as mybir
from concourse.bass_utils import run_bass_kernel_spmd
from contextlib import ExitStack

F32 = mybir.dt.float32
BF16 = mybir.dt.bfloat16
FP16 = mybir.dt.float16
I16 = mybir.dt.int16
U8 = mybir.dt.uint8
AF = mybir.ActivationFunctionType
AXL = mybir.AxisListType
ALU = mybir.AluOpType

S = 2048
D = 2048
HN = 16
KVH = 4
I_FF = 5632
GU = 2 * I_FF
CC = 16
TTN = 16
NPASS = 4
TPP = 4
TPASS = 512
ICN = 44
OGN = 4
JT_GU = 88
OT_DN = 16


def build_program(dbg=False, upto=None):
    nc = bass.Bass("TRN2", target_bir_lowering=False, detect_race_conditions=False)

    x_d = nc.dram_tensor("x", [S, D], F32, kind="ExternalInput").ap()
    wq_d = nc.dram_tensor("wq_t", [D, D], F32, kind="ExternalInput").ap()
    wk_d = nc.dram_tensor("wk_t", [D, 512], F32, kind="ExternalInput").ap()
    wv_d = nc.dram_tensor("wv_t", [D, 512], F32, kind="ExternalInput").ap()
    wo_d = nc.dram_tensor("wo_t", [D, D], F32, kind="ExternalInput").ap()
    bq_d = nc.dram_tensor("bq", [D], F32, kind="ExternalInput").ap()
    bk_d = nc.dram_tensor("bk", [512], F32, kind="ExternalInput").ap()
    bv_d = nc.dram_tensor("bv", [512], F32, kind="ExternalInput").ap()
    gup_d = nc.dram_tensor("gup", [11 * 128, D // 2], U8, kind="ExternalInput").ap()
    dnp_d = nc.dram_tensor("dnp", [2 * 128, I_FF // 2], U8, kind="ExternalInput").ap()
    gus_d = nc.dram_tensor("gus", [CC, 11 * 128], FP16, kind="ExternalInput").ap()
    dns_d = nc.dram_tensor("dns", [ICN, 2 * 128], FP16, kind="ExternalInput").ap()
    idb_d = nc.dram_tensor("idb", [128, 128], BF16, kind="ExternalInput").ap()
    idh_d = nc.dram_tensor("idh", [128, 128], FP16, kind="ExternalInput").ap()
    out_d = nc.dram_tensor("out", [S, D], F32, kind="ExternalOutput").ap()

    gu_loc = nc.dram_tensor("gu_loc", [CC * 128, 11 * 128], BF16).ap()
    dn_loc = nc.dram_tensor("dn_loc", [ICN * 128, 2 * 128], BF16).ap()
    gu_ag = nc.dram_tensor("gu_ag", [8 * CC * 128, 11 * 128], BF16, addr_space="Shared").ap()
    dn_ag = nc.dram_tensor("dn_ag", [8 * ICN * 128, 2 * 128], BF16, addr_space="Shared").ap()
    bar_in = nc.dram_tensor("bar_in", [1, 16], F32).ap()
    bar_out = nc.dram_tensor("bar_out", [1, 16], F32, addr_space="Shared").ap()
    xn_sc = nc.dram_tensor("xn_sc", [CC, 128, S], BF16).ap()

    if dbg:
        dbg_gu = nc.dram_tensor("dbg_gu", [8 * CC * 128, 11 * 128], BF16, kind="ExternalOutput").ap()
        dbg_dn = nc.dram_tensor("dbg_dn", [8 * ICN * 128, 2 * 128], BF16, kind="ExternalOutput").ap()
        dbg_xn = nc.dram_tensor("dbg_xn", [CC, 128, S], BF16, kind="ExternalOutput").ap()
        dbg_kv = nc.dram_tensor("dbg_kv", [128, 512], BF16, kind="ExternalOutput").ap()
        dbg_ks = nc.dram_tensor("dbg_ks", [128, 4], BF16, kind="ExternalOutput").ap()
        dbg_attn = nc.dram_tensor("dbg_attn", [128, HN * 512], BF16, kind="ExternalOutput").ap()
        dbg_h = nc.dram_tensor("dbg_h", [128, ICN * 512], BF16, kind="ExternalOutput").ap()
        dbg_y = nc.dram_tensor("dbg_y", [128, TPP * D], F32, kind="ExternalOutput").ap()
        dbg_st = nc.dram_tensor("dbg_st", [128, 96], F32, kind="ExternalOutput").ap()
        dbg_gup = nc.dram_tensor("dbg_gup", [128, 1536], F32, kind="ExternalOutput").ap()
        dbg_xnt = nc.dram_tensor("dbg_xnt", [128, D], BF16, kind="ExternalOutput").ap()

    ctx = ExitStack()
    with ctx:
        sbuf = lambda n, sh, dt: ctx.enter_context(nc.sbuf_tensor(n, sh, dt))
        psum = lambda n, sh, dt: ctx.enter_context(nc.psum_tensor(n, sh, dt))
        sem = lambda n: ctx.enter_context(nc.semaphore(n))

        bq_sb = sbuf("bq_sb", [128, CC], F32)
        bq1_sb = sbuf("bq1_sb", [128, CC], F32)
        bkv_sb = sbuf("bkv_sb", [1, 1024], BF16)
        ones_col = sbuf("ones_col", [128, 1], BF16)
        ones_row = sbuf("ones_row", [1, 128], BF16)
        ones_row_f = sbuf("ones_row_f", [1, 128], F32)
        idb_sb = sbuf("idb_sb", [128, 128], BF16)
        kv_sb = sbuf("kv_sb", [128, 512], BF16)
        eps5_sb = sbuf("eps5_sb", [128, 1], F32)
        eps6_sb = sbuf("eps6_sb", [128, 1], F32)
        elu_a = sbuf("elu_a", [128, 512], F32)
        elu_b = sbuf("elu_b", [128, 512], F32)
        ksum_sb = sbuf("ksum_sb", [128, 4], BF16)

        pA = psum("pA", [128, 512], F32)
        pB = psum("pB", [128, 512], F32)
        pC = psum("pC", [128, 512], F32)
        pD = psum("pD", [128, 512], F32)
        pE = psum("pE", [128, 512], F32)
        pF = psum("pF", [128, 512], F32)
        pT = psum("pT", [128, 1024], BF16)
        pS = psum("pS", [128, 512], FP16)
        PQ = (pA, pB, pC, pD)

        with nc.Block("init", no_gpsimd_drain=True) as blk:
            i_s = sem("i_s")
            i_g = sem("i_g")
            i_m = sem("i_m")

            @blk.sync
            def _(sy):
                sy.dma_start(out=bq_sb[:], in_=bq_d.rearrange("(k p) -> p k", p=128),
                             allow_slow_non_contiguous=True).then_inc(i_s, 16)
                sy.dma_start(out=idb_sb[:], in_=idb_d).then_inc(i_s, 16)
                sy.wait_ge(i_s, 32)

            @blk.gpsimd
            def _(g):
                g.dma_start(out=bkv_sb[0:1, 0:512],
                            in_=bk_d.rearrange("(o f) -> o f", o=1)).then_inc(i_g, 16)
                g.dma_start(out=bkv_sb[0:1, 512:1024],
                            in_=bv_d.rearrange("(o f) -> o f", o=1)).then_inc(i_g, 16)
                g.memset(ones_col[:], 1.0)
                g.memset(eps5_sb[:], 1e-5)
                g.memset(eps6_sb[:], 1e-6)
                g.memset(ones_row_f[:], 1.0)
                g.memset(ones_row[:], 1.0).then_inc(i_m, 1)
                g.wait_ge(i_g, 32)

            @blk.vector
            def _(v):
                v.wait_ge(i_s, 16)
                v.tensor_scalar(bq1_sb[:], bq_sb[:], 1.0, None, ALU.add)
                v.wait_ge(i_m, 1)

        cc_sem = sem("cc_sem")
        if upto == "init":
            return nc

        tiles = [("gu", jt) for jt in range(11)] + [("dn", ot) for ot in range(2)]
        nbat = [2 if k == "gu" else 6 for k, _ in tiles]
        cumbat = [0]
        for nb_ in nbat:
            cumbat.append(cumbat[-1] + nb_)

        pctx = ExitStack()
        psb = lambda n, sh, dt: pctx.enter_context(nc.sbuf_tensor(n, sh, dt))
        idh_sb = psb("idh_sb", [128, 128], FP16)
        gus_sb = psb("gus_sb", [CC, 11 * 128], FP16)
        dns_sb = psb("dns_sb", [ICN, 2 * 128], FP16)
        MAXB = I_FF // 2
        byt = [psb(f"byt{i}", [128, MAXB], U8) for i in range(2)]
        v16 = psb("v16", [128, MAXB], I16)
        pl_lo = psb("pl_lo", [128, MAXB], I16)
        pl_hi = psb("pl_hi", [128, MAXB], I16)
        sgn = psb("sgn", [128, MAXB], I16)
        cmb = psb("cmb", [128, MAXB], I16)
        wdq = [psb(f"wdq{i}", [128, I_FF], BF16) for i in range(2)]
        wtp = [psb(f"wtp{i}", [128, I_FF], BF16) for i in range(2)]
        sc_col = psb("sc_col", [128, ICN], F32)
        x_t = [psb(f"x_t{i}", [128, D], F32) for i in range(2)]
        xn_tok = [psb(f"xn_tok{i}", [128, D], BF16) for i in range(2)]
        stage_x = [psb(f"stage_x{i}", [128, D], BF16) for i in range(2)]
        st_s1 = psb("st_s1", [128, 1], F32)
        st_s2 = psb("st_s2", [128, 1], F32)
        st_m = psb("st_m", [128, 1], F32)
        st_t2 = psb("st_t2", [128, 1], F32)
        st_var = psb("st_var", [128, 1], F32)
        st_ln = psb("st_ln", [128, 1], F32)
        st_rstd = psb("st_rstd", [128, 1], F32)
        st_nmr = psb("st_nmr", [128, 1], F32)
        ln_f32 = psb("ln_f32", [128, D], F32)
        xsq_sb = psb("xsq_sb", [128, D], F32)
        wk_sb = psb("wk_sb", [128, CC * 512], BF16)
        wv_sb = psb("wv_sb", [128, CC * 512], BF16)
        sk_tok = [psb(f"sk_tok{i}", [128, 512], BF16) for i in range(2)]
        v_tok = [psb(f"v_tok{i}", [128, 512], BF16) for i in range(2)]
        if dbg:
            dbgst_sb = psb("dbgst_sb", [128, 96], F32)

        pDQ = (pC.ap().bitcast(BF16), pD.ap().bitcast(BF16))

        with nc.Block("prep", no_gpsimd_drain=True) as blk:
            d_ld = sem("d_ld")
            d_sc = sem("d_sc")
            d_se = sem("d_se")
            d_dv = sem("d_dv")
            d_pe = sem("d_pe")
            d_ev = sem("d_ev")
            d_wr = sem("d_wr")
            l_xd = sem("l_xd")
            l_st = sem("l_st")
            l_vs = sem("l_vs")
            l_ls = sem("l_ls")
            l_xn = sem("l_xn")
            l_tr = sem("l_tr")
            l_ev = sem("l_ev")
            l_xw = sem("l_xw")
            l_wk = sem("l_wk")
            k_mm = sem("k_mm")
            k_ep = sem("k_ep")
            k_ee = sem("k_ee")
            k_kv = sem("k_kv")
            k_fin = sem("k_fin")
            c_v = sem("c_v")
            c_s = sem("c_s")
            cv = [0]
            cs = [0]

            @blk.sync
            def _(sy):
                sy.dma_start(out=gus_sb[:], in_=gus_d).then_inc(d_ld, 16)
                sy.dma_start(out=dns_sb[:], in_=dns_d).then_inc(d_ld, 16)
                sy.dma_start(out=idh_sb[:], in_=idh_d).then_inc(d_ld, 16)
                for s in range(TTN + 1):
                    if s < len(tiles):
                        kind, it = tiles[s]
                        nb = 1024 if kind == "gu" else 2816
                        srcb = (gup_d if kind == "gu" else dnp_d)[it * 128:(it + 1) * 128, :]
                        if s >= 2:
                            sy.wait_ge(d_dv, s - 1)
                        sy.dma_start(out=byt[s % 2][:, 0:nb], in_=srcb).then_inc(d_ld, 16)
                    if 1 <= s <= len(tiles):
                        j = s - 1
                        kindj, itj = tiles[j]
                        npc = 16 if kindj == "gu" else 44
                        sy.wait_ge(d_ev, cumbat[j + 1])
                        if kindj == "gu":
                            dst = gu_loc.rearrange("(cc c) j -> c cc j", c=128)[:, :, itj * 128:(itj + 1) * 128]
                        else:
                            dst = dn_loc.rearrange("(ic i) o -> i ic o", i=128)[:, :, itj * 128:(itj + 1) * 128]
                        sy.dma_start(out=dst, in_=wtp[j % 2][:, 0:npc * 128]).then_inc(d_wr, 16)
                    if s < TTN:
                        sy.wait_ge(l_ev, 2 * (s + 1))
                        dst = xn_sc.rearrange("cc c t -> c cc t")[:, :, s * 128:(s + 1) * 128]
                        sy.dma_start(out=dst, in_=stage_x[s % 2][:]).then_inc(l_xw, 16)
                sy.wait_ge(d_wr, 16 * len(tiles))
                sy.wait_ge(l_xw, 16 * TTN)

            @blk.gpsimd
            def _(g):
                for s in range(2):
                    g.dma_start(out=x_t[s][:],
                                in_=x_d[s * 128:(s + 1) * 128, :]).then_inc(l_xd, 16)
                for cc in range(CC):
                    g.dma_start(out=wk_sb[:, cc * 512:(cc + 1) * 512],
                                in_=wk_d[cc * 128:(cc + 1) * 128, :]).then_inc(l_wk, 16)
                    g.dma_start(out=wv_sb[:, cc * 512:(cc + 1) * 512],
                                in_=wv_d[cc * 128:(cc + 1) * 128, :]).then_inc(l_wk, 16)
                for s in range(2, TTN):
                    g.wait_ge(l_xn, s - 1)
                    g.dma_start(out=x_t[s % 2][:],
                                in_=x_d[s * 128:(s + 1) * 128, :]).then_inc(l_xd, 16)
                g.wait_ge(d_wr, 16 * len(tiles))
                g.collective_compute(
                    "AllGather", ALU.bypass, replica_groups=[list(range(8))],
                    ins=[gu_loc.opt()], outs=[gu_ag.opt()]).then_inc(cc_sem, 1)
                g.collective_compute(
                    "AllGather", ALU.bypass, replica_groups=[list(range(8))],
                    ins=[dn_loc.opt()], outs=[dn_ag.opt()]).then_inc(cc_sem, 1)
                g.memset(sc_col[0:1, 0:16], 0.0).then_inc(cc_sem, 1)
                g.wait_ge(cc_sem, 3)
                g.dma_start(out=bar_in, in_=sc_col[0:1, 0:16]).then_inc(cc_sem, 16)
                g.wait_ge(cc_sem, 19)
                g.collective_compute(
                    "AllReduce", ALU.add, replica_groups=[list(range(8))],
                    ins=[bar_in.opt()], outs=[bar_out.opt()]).then_inc(cc_sem, 1)

            @blk.scalar
            def _(s):
                for st in range(TTN):
                    s.wait_ge(l_xd, 16 * (st + 1))
                    s.activation(xsq_sb[:], x_t[st % 2][:], AF.Square).then_inc(l_st, 1)
                    s.wait_ge(l_vs, st + 1)
                    inst = s.activation(st_ln[:, 0:1], st_var[:, 0:1], AF.Ln, bias=eps5_sb[:, 0:1])
                    inst.then_inc(c_s, 1)
                    cs[0] += 1
                    s.wait_ge(c_s, cs[0])
                    s.activation(st_rstd[:, 0:1], st_ln[:, 0:1], AF.Exp,
                                 scale=-0.5).then_inc(l_ls, 1)
                    if st < len(tiles):
                        ng = 16 if tiles[st][0] == "gu" else 44
                        s.wait_ge(d_sc, st + 1)
                        s.mul(sc_col[:, 0:ng], pS[:, 0:ng], float(2.0 ** 14)).then_inc(d_se, 1)
                    if st >= 2:
                        s.wait_ge(l_xw, 16 * (st - 1))
                        s.wait_ge(k_mm, st - 1)
                    for b in range(2):
                        s.wait_ge(l_tr, 2 * st + b + 1)
                        s.copy(stage_x[st % 2][:, b * 1024:(b + 1) * 1024],
                               pT[:]).then_inc(l_ev, 1)
                    if st >= 1:
                        s.wait_ge(k_ep, 3 * (st - 1) + 1)
                        s.activation(elu_b[:], elu_a[:], AF.Exp).then_inc(k_ee, 1)
                    if 1 <= st <= len(tiles):
                        j = st - 1
                        ng = 16 if tiles[j][0] == "gu" else 44
                        if j >= 2:
                            s.wait_ge(d_wr, 16 * (j - 1))
                        for b in range(nbat[j]):
                            gb = cumbat[j] + b
                            s.wait_ge(d_pe, gb + 1)
                            lo, hi = b * 8, min(b * 8 + 8, ng)
                            s.copy(wtp[j % 2][:, lo * 128:hi * 128],
                                   pDQ[gb % 2][:, 0:(hi - lo) * 128]).then_inc(d_ev, 1)
                s.wait_ge(k_ep, 3 * (TTN - 1) + 1)
                s.activation(elu_b[:], elu_a[:], AF.Exp).then_inc(k_ee, 1)

            @blk.vector
            def _(v):
                def vs(inst):
                    inst.then_inc(c_v, 1)
                    cv[0] += 1
                    v.wait_ge(c_v, cv[0])
                for st in range(TTN):
                    v.wait_ge(l_st, st + 1)
                    vs(v.reduce_sum(st_s1[:, 0:1], x_t[st % 2][:], axis=AXL.X))
                    vs(v.reduce_sum(st_s2[:, 0:1], xsq_sb[:], axis=AXL.X))
                    vs(v.tensor_scalar(st_m[:, 0:1], st_s1[:, 0:1], 1.0 / D, None, ALU.mult))
                    vs(v.tensor_scalar(st_t2[:, 0:1], st_s2[:, 0:1], 1.0 / D, None, ALU.mult))
                    vs(v.tensor_tensor(st_var[:, 0:1], st_m[:, 0:1], st_m[:, 0:1], ALU.mult))
                    vs(v.tensor_tensor(st_var[:, 0:1], st_t2[:, 0:1], st_var[:, 0:1],
                                       ALU.subtract))
                    v.sem_inc(l_vs, 1)
                    v.wait_ge(l_ls, st + 1)
                    vs(v.tensor_scalar(st_nmr[:, 0:1], st_m[:, 0:1], -1.0, None, ALU.mult))
                    vs(v.tensor_tensor(st_nmr[:, 0:1], st_nmr[:, 0:1], st_rstd[:, 0:1],
                                       ALU.mult))
                    if st >= 2:
                        v.wait_ge(l_tr, 2 * (st - 1))
                    v.tensor_scalar(ln_f32[:], x_t[st % 2][:],
                                    st_nmr[:, 0:1], None, ALU.add)
                    v.tensor_scalar(xn_tok[st % 2][:], ln_f32[:],
                                    st_rstd[:, 0:1], None, ALU.mult)
                    if dbg:
                        v.tensor_copy(dbgst_sb[:, st:st + 1], st_s1[:, 0:1])
                        v.tensor_copy(dbgst_sb[:, 16 + st:17 + st], st_var[:, 0:1])
                        v.tensor_copy(dbgst_sb[:, 32 + st:33 + st], st_rstd[:, 0:1])
                    v.tensor_copy(st_s1[0:1, 0:1], st_s1[0:1, 0:1]).then_inc(l_xn, 1)
                    if st >= 1:
                        j = st - 1
                        v.wait_ge(k_mm, j + 1)
                        if j >= 2:
                            v.wait_ge(k_kv, j - 1)
                        v.tensor_scalar(elu_a[:], pA[:], 0.0, None, ALU.min).then_inc(k_ep, 1)
                        v.tensor_copy(v_tok[j % 2][:], pB[:]).then_inc(k_ep, 1)
                        v.wait_ge(k_ee, j + 1)
                        v.scalar_tensor_tensor(sk_tok[j % 2][:], pA[:], 1.0, elu_b[:],
                                               ALU.add, ALU.max).then_inc(k_ep, 1)
                    if st < len(tiles):
                        i = st
                        kind, it = tiles[i]
                        ng = 16 if kind == "gu" else 44
                        nb = 64 * ng
                        v.wait_ge(d_ld, 16 * (i + 4))
                        v.wait_ge(d_se, i + 1)
                        if i >= 2:
                            v.wait_ge(d_pe, cumbat[i - 1])
                        v.tensor_copy(v16[:, 0:nb], byt[i % 2][:, 0:nb])
                        v.tensor_scalar(pl_lo[:, 0:nb], v16[:, 0:nb], 15, 9,
                                        ALU.bitwise_and, ALU.logical_shift_left)
                        v.tensor_scalar(pl_hi[:, 0:nb], v16[:, 0:nb], 0xF0, 5,
                                        ALU.bitwise_and, ALU.logical_shift_left)
                        wint = wdq[i % 2].ap().rearrange("p (c two) -> p c two", two=2)
                        last = None
                        for plane, pl in ((0, pl_lo), (1, pl_hi)):
                            v.tensor_scalar(sgn[:, 0:nb], pl[:, 0:nb], 0x1000, 3,
                                            ALU.bitwise_and, ALU.logical_shift_left)
                            v.tensor_scalar(cmb[:, 0:nb], pl[:, 0:nb], 0x0E00, None, ALU.bitwise_and)
                            v.tensor_tensor(cmb[:, 0:nb], cmb[:, 0:nb], sgn[:, 0:nb], ALU.bitwise_or)
                            cview = cmb.ap().bitcast(FP16)
                            for gsl in range(ng):
                                last = v.tensor_scalar(
                                    wint[:, 64 * gsl:64 * (gsl + 1), plane:plane + 1],
                                    cview[:, 64 * gsl:64 * (gsl + 1)],
                                    sc_col[:, gsl:gsl + 1], None, ALU.mult)
                        last.then_inc(d_dv, 1)
                j = TTN - 1
                v.wait_ge(k_mm, j + 1)
                v.wait_ge(k_kv, j - 1)
                v.tensor_scalar(elu_a[:], pA[:], 0.0, None, ALU.min).then_inc(k_ep, 1)
                v.tensor_copy(v_tok[j % 2][:], pB[:]).then_inc(k_ep, 1)
                v.wait_ge(k_ee, j + 1)
                v.scalar_tensor_tensor(sk_tok[j % 2][:], pA[:], 1.0, elu_b[:],
                                       ALU.add, ALU.max).then_inc(k_ep, 1)
                v.wait_ge(k_kv, TTN)
                v.tensor_copy(kv_sb[:], pE[:])
                v.tensor_copy(ksum_sb[:], pF[:, 0:4]).then_inc(k_fin, 1)

            @blk.tensor
            def _(t):
                t.wait_ge(d_ld, 48)
                for st in range(TTN):
                    if st < len(tiles):
                        ng = 16 if tiles[st][0] == "gu" else 44
                        if st >= 1:
                            t.wait_ge(d_se, st)
                        ssrc = (gus_sb if tiles[st][0] == "gu" else dns_sb)[
                            0:ng, tiles[st][1] * 128:(tiles[st][1] + 1) * 128]
                        t.transpose(pS[:, 0:ng], ssrc, idh_sb[0:ng, 0:ng]).then_inc(d_sc, 1)
                    t.wait_ge(l_xn, st + 1)
                    for b in range(2):
                        gb = 2 * st + b
                        if gb >= 1:
                            t.wait_ge(l_ev, gb)
                        mm = None
                        for k in range(8):
                            cc = b * 8 + k
                            mm = t.transpose(pT[:, k * 128:(k + 1) * 128],
                                             xn_tok[st % 2][:, cc * 128:(cc + 1) * 128],
                                             idb_sb[:])
                        mm.then_inc(l_tr, 1)
                    t.wait_ge(l_ev, 2 * (st + 1))
                    if st == 0:
                        t.wait_ge(l_wk, 16 * 2 * CC)
                    if st >= 1:
                        t.wait_ge(k_ep, 3 * st)
                    for cc in range(CC):
                        t.matmul(pA[:], stage_x[st % 2][:, cc * 128:(cc + 1) * 128],
                                 wk_sb[:, cc * 512:(cc + 1) * 512], start=(cc == 0), stop=False)
                    t.matmul(pA[:], ones_row[0:1, :], bkv_sb[0:1, 0:512],
                             start=False, stop=True)
                    for cc in range(CC):
                        t.matmul(pB[:], stage_x[st % 2][:, cc * 128:(cc + 1) * 128],
                                 wv_sb[:, cc * 512:(cc + 1) * 512], start=(cc == 0), stop=False)
                    t.matmul(pB[:], ones_row[0:1, :], bkv_sb[0:1, 512:1024],
                             start=False, stop=True).then_inc(k_mm, 1)
                    if st >= 1:
                        j = st - 1
                        t.wait_ge(k_ep, 3 * (j + 1))
                        mm = None
                        for gq in range(KVH):
                            t.matmul(pE[:, gq * 128:(gq + 1) * 128],
                                     sk_tok[j % 2][:, gq * 128:(gq + 1) * 128],
                                     v_tok[j % 2][:, gq * 128:(gq + 1) * 128],
                                     start=(j == 0 and gq == 0), stop=False,
                                     skip_group_check=True)
                            mm = t.matmul(pF[:, gq:gq + 1],
                                          sk_tok[j % 2][:, gq * 128:(gq + 1) * 128],
                                          ones_col[:, 0:1],
                                          start=(j == 0 and gq == 0), stop=False,
                                          skip_group_check=True)
                        mm.then_inc(k_kv, 1)
                    if 1 <= st <= len(tiles):
                        j = st - 1
                        ng = 16 if tiles[j][0] == "gu" else 44
                        t.wait_ge(d_dv, j + 1)
                        for b in range(nbat[j]):
                            gb = cumbat[j] + b
                            if gb >= 2:
                                t.wait_ge(d_ev, gb - 1)
                            lo, hi = b * 8, min(b * 8 + 8, ng)
                            mm = None
                            for k in range(lo, hi):
                                mm = t.transpose(pDQ[gb % 2][:, (k - lo) * 128:(k - lo + 1) * 128],
                                                 wdq[j % 2][:, k * 128:(k + 1) * 128], idb_sb[:])
                            mm.then_inc(d_pe, 1)
                j = TTN - 1
                t.wait_ge(k_ep, 3 * (j + 1))
                mm = None
                for gq in range(KVH):
                    t.matmul(pE[:, gq * 128:(gq + 1) * 128],
                             sk_tok[j % 2][:, gq * 128:(gq + 1) * 128],
                             v_tok[j % 2][:, gq * 128:(gq + 1) * 128],
                             start=False, stop=True, skip_group_check=True)
                    mm = t.matmul(pF[:, gq:gq + 1],
                                  sk_tok[j % 2][:, gq * 128:(gq + 1) * 128],
                                  ones_col[:, 0:1],
                                  start=False, stop=True, skip_group_check=True)
                mm.then_inc(k_kv, 1)

            if dbg:
                p_dbg0 = sem("p_dbg0")

                @blk.scalar
                def _(s):
                    s.wait_ge(cc_sem, 20)
                    s.dma_start(out=dbg_gu, in_=gu_ag).then_inc(p_dbg0, 16)
                    s.dma_start(out=dbg_dn, in_=dn_ag).then_inc(p_dbg0, 16)
                    s.wait_ge(l_xw, 16 * TTN)
                    s.dma_start(out=dbg_xn, in_=xn_sc).then_inc(p_dbg0, 16)
                    s.dma_start(out=dbg_st, in_=dbgst_sb[:]).then_inc(p_dbg0, 16)
                    s.wait_ge(p_dbg0, 64)

                @blk.gpsimd
                def _(g):
                    g.wait_ge(k_fin, 1)
                    g.dma_start(out=dbg_kv, in_=kv_sb[:]).then_inc(p_dbg0, 16)
                    g.dma_start(out=dbg_ks, in_=ksum_sb[:]).then_inc(p_dbg0, 16)
                    g.wait_ge(p_dbg0, 96)

        pctx.close()
        if upto in ("deq", "ln", "kv", "prep"):
            return nc

        xn_buf = sbuf("xn_buf", [128, CC * 512], BF16)
        out_acc = [sbuf(f"oacc{i}", [128, D], F32) for i in range(TPP)]
        h_sb = sbuf("h_sb", [128, ICN * 512], BF16)
        u0 = sbuf("u0", [128, TPP * D], F32)
        wq_v = u0.ap().bitcast(BF16)
        gu_buf = [sbuf(f"gu_buf{i}", [128, CC * 128], BF16) for i in range(3)]
        dn_buf = [sbuf(f"dn_buf{i}", [128, 11 * 512], BF16) for i in range(2)]
        wo_buf = [sbuf(f"wo_buf{i}", [128, 4 * 512], BF16) for i in range(2)]
        attn_fm = sbuf("attn_fm", [128, HN * 512], BF16)
        sq_buf = sbuf("sq_buf", [128, 4 * 512], BF16)
        rn_f32 = sbuf("rn_f32", [1, 512], F32)
        rn_bf = sbuf("rn_bf", [1, 512], BF16)
        rn_bc = sbuf("rn_bc", [128, 512], BF16)
        sg_buf = [sbuf(f"sg_buf{i}", [128, 512], F32) for i in range(2)]
        s2_col = sbuf("s2_col", [128, 1], F32)
        r2_ln = sbuf("r2_ln", [128, 1], F32)
        r2_col = sbuf("r2_col", [128, 1], F32)

        p_xn = sem("p_xn"); T_xn = 16 * CC
        p_wq = sem("p_wq"); T_wq = 16 * 4
        p_q = sem("p_q"); T_q = 4
        p_sq = sem("p_sq"); T_sq = 32
        p_ex = sem("p_ex"); T_ex = 16
        p_hd = sem("p_hd"); T_hd = 48
        p_hv = sem("p_hv"); T_hv = 32
        p_he = sem("p_he"); T_he = 16
        p_xa = sem("p_xa"); T_xa = 16 * TPP
        p_wo = sem("p_wo"); T_wo = 16 * OGN * 4
        p_wu = sem("p_wu"); T_wu = OGN * 4
        p_oa = sem("p_oa"); T_oa = OGN * TPP
        p_gl = sem("p_gl"); T_gl = 16 * 2 * ICN
        p_gm = sem("p_gm"); T_gm = 2 * ICN
        p_sl = sem("p_sl"); T_sl = ICN
        p_hh = sem("p_hh"); T_hh = ICN
        p_dl = sem("p_dl"); T_dl = 32 * 16
        p_dm = sem("p_dm"); T_dm = 16
        p_ye = sem("p_ye"); T_ye = OGN * TPP
        p_rs = sem("p_rs"); T_rs = TPP
        p_r1 = sem("p_r1"); T_r1 = TPP
        p_r2 = sem("p_r2"); T_r2 = TPP
        pc_v = sem("pc_v")
        pc_s = sem("pc_s")
        pcv = [0]
        pcs = [0]
        p_fin = sem("p_fin"); T_fin = TPP
        p_out = sem("p_out"); T_out = 16 * TPP
        p_dbg = sem("p_dbg")

        npass_build = NPASS
        if upto and upto.startswith("pass"):
            npass_build = int(upto[4:]) + 1
        for p in range(npass_build):
            B_xn = p * T_xn; B_wq = p * T_wq; B_q = p * T_q; B_sq = p * T_sq
            B_ex = p * T_ex; B_hd = p * T_hd; B_hv = p * T_hv; B_he = p * T_he
            B_xa = p * T_xa; B_wo = p * T_wo; B_wu = p * T_wu
            B_oa = p * T_oa; B_gl = p * T_gl; B_gm = p * T_gm; B_sl = p * T_sl
            B_hh = p * T_hh; B_dl = p * T_dl; B_dm = p * T_dm; B_ye = p * T_ye
            B_rs = p * T_rs; B_fin = p * T_fin; B_out = p * T_out
            B_r1 = p * T_r1; B_r2 = p * T_r2
            tok0 = p * TPASS

            with nc.Block(f"pass{p}", no_gpsimd_drain=True) as blk:

                @blk.sync
                def _(sy, tok0=tok0, B_gm=B_gm, B_dm=B_dm, B_r1=B_r1, B_r2=B_r2, p=p):
                    if p > 0:
                        sy.wait_ge(p_gm, B_gm)
                    for cc in range(CC):
                        sy.dma_start(out=xn_buf[:, cc * 512:(cc + 1) * 512],
                                     in_=xn_sc[cc, :, tok0:tok0 + TPASS]).then_inc(p_xn, 16)
                    if p == 0:
                        sy.wait_ge(cc_sem, 20)
                    for hc in range(ICN):
                        for half in range(2):
                            gi = 2 * hc + half
                            jt = hc + half * ICN
                            rk, jrt = jt // 11, jt % 11
                            if gi >= 3:
                                sy.wait_ge(p_gm, B_gm + gi - 2)
                            src = gu_ag[rk * CC * 128:(rk + 1) * CC * 128, :].rearrange(
                                "(cc c) j -> c cc j", c=128)[:, :, jrt * 128:(jrt + 1) * 128]
                            sy.dma_start(out=gu_buf[gi % 3][:], in_=src).then_inc(p_gl, 16)
                    for og in range(OGN):
                        for icq in range(4):
                            idx = og * 4 + icq
                            if idx >= 2:
                                sy.wait_ge(p_dm, B_dm + idx - 1)
                            elif p > 0:
                                sy.wait_ge(p_r1, B_r1)
                                sy.wait_ge(p_r2, B_r2)
                            for rr in range(2):
                                srcr = dn_ag.rearrange("(r ic i) o -> i ic r o", ic=ICN, i=128)[
                                    :, icq * 11:(icq + 1) * 11, 2 * og + rr, :]
                                dstv = dn_buf[idx % 2].ap().rearrange(
                                    "p (ic r o) -> p ic r o", ic=11, r=2, o=256)[:, :, rr, :]
                                sy.dma_start(out=dstv, in_=srcr).then_inc(p_dl, 16)

                @blk.scalar
                def _(s, tok0=tok0, B_sq=B_sq, B_hd=B_hd, B_gm=B_gm, B_ye=B_ye,
                      B_fin=B_fin, B_out=B_out, B_r2=B_r2, p=p):
                    for tt in range(TPP):
                        row = tok0 + tt * 128
                        s.dma_start(out=out_acc[tt][:],
                                    in_=x_d[row:row + 128, :]).then_inc(p_xa, 16)
                    for dqg in range(4):
                        for dq in range(4):
                            ti = dqg * 4 + dq
                            s.wait_ge(p_sq, B_sq + 2 * ti + 1)
                            s.activation(elu_b[:], elu_a[:], AF.Exp).then_inc(p_ex, 1)
                        for dq in range(4):
                            hh = dqg * 4 + dq
                            s.wait_ge(p_hd, B_hd + 3 * hh + 2)
                            s.copy(rn_bc[:], pF[:]).then_inc(p_he, 1)
                    for hc in range(ICN):
                        s.wait_ge(p_gm, B_gm + 2 * hc + 1)
                        psg = pA if hc % 2 == 0 else pC
                        s.activation(sg_buf[hc % 2][:], psg[:], AF.Silu).then_inc(p_sl, 1)
                    for tt in range(TPP):
                        s.wait_ge(p_ye, B_ye + 3 * TPP + tt + 1)
                        if p > 0 or tt >= 1:
                            s.wait_ge(p_fin, B_fin + tt)
                        yjunk = dn_buf[0].ap().bitcast(F32)
                        s.activation(yjunk[:, 0:D], u0[:, tt * D:(tt + 1) * D],
                                     AF.Square).then_inc(p_r1, 1)
                        s.wait_ge(p_r2, B_r2 + tt + 1)
                        inst = s.activation(r2_ln[:, 0:1], s2_col[:, 0:1], AF.Ln,
                                            scale=1.0 / D, bias=eps6_sb[:, 0:1])
                        inst.then_inc(pc_s, 1); pcs[0] += 1; s.wait_ge(pc_s, pcs[0])
                        s.activation(r2_col[:, 0:1], r2_ln[:, 0:1], AF.Exp,
                                     scale=-0.5).then_inc(p_rs, 1)
                    for tt in range(TPP):
                        s.wait_ge(p_fin, B_fin + tt + 1)
                        row = tok0 + tt * 128
                        s.dma_start(out=out_d[row:row + 128, :],
                                    in_=out_acc[tt][:]).then_inc(p_out, 16)
                    if p == NPASS - 1:
                        s.wait_ge(p_out, B_out + 16 * TPP)

                @blk.gpsimd
                def _(g, p=p, B_q=B_q, B_wu=B_wu, B_hv=B_hv, B_hh=B_hh, B_ye=B_ye):
                    for dqg in range(4):
                        if p > 0 and dqg < 2:
                            g.wait_ge(p_fin, B_fin - TPP + 2 * (dqg + 1))
                        if dqg >= 2:
                            g.wait_ge(p_q, B_q + dqg - 1)
                        half_off = (dqg % 2) * (CC * 512)
                        dstv = wq_v[:, half_off:half_off + CC * 512].rearrange(
                            "p (cc f) -> p cc f", cc=CC)
                        srcv = wq_d.rearrange("(cc c) q -> c cc q", c=128)[
                            :, :, dqg * 512:(dqg + 1) * 512]
                        g.dma_start(out=dstv, in_=srcv).then_inc(p_wq, 16)
                    for og in range(OGN):
                        for hq in range(4):
                            idx = og * 4 + hq
                            if idx >= 2:
                                g.wait_ge(p_wu, B_wu + idx - 1)
                            elif p > 0:
                                g.wait_ge(p_wu, B_wu + idx - 1)
                            dstv = wo_buf[idx % 2].ap().rearrange("p (hh f) -> p hh f", hh=4)
                            srcv = wo_d.rearrange("(hh h) q -> h hh q", h=128)[
                                :, hq * 4:(hq + 1) * 4, og * 512:(og + 1) * 512]
                            g.dma_start(out=dstv, in_=srcv).then_inc(p_wo, 16)
                    if dbg and p == 0:
                        g.wait_ge(p_hv, B_hv + 2 * HN)
                        g.dma_start(out=dbg_attn, in_=attn_fm[:]).then_inc(p_dbg, 16)
                        g.wait_ge(p_hh, B_hh + 1)
                        g.dma_start(out=dbg_gup, in_=u0[:, 0:1536]).then_inc(p_dbg, 16)
                        g.wait_ge(p_hh, B_hh + ICN)
                        g.dma_start(out=dbg_h, in_=h_sb[:]).then_inc(p_dbg, 16)
                        g.wait_ge(p_ye, B_ye + OGN * TPP)
                        g.dma_start(out=dbg_y, in_=u0[:]).then_inc(p_dbg, 16)
                        g.wait_ge(p_dbg, 64)

                @blk.tensor
                def _(t, B_xn=B_xn, B_wq=B_wq, B_sq=B_sq, B_hv=B_hv, B_he=B_he,
                      B_wo=B_wo, B_oa=B_oa, B_hh=B_hh, B_gl=B_gl, B_dl=B_dl,
                      B_ye=B_ye, p=p):
                    t.wait_ge(p_xn, B_xn + 16 * CC)
                    for step in range(5):
                        if step < 4:
                            dqg = step
                            t.wait_ge(p_wq, B_wq + 16 * (dqg + 1))
                            if dqg >= 1:
                                t.wait_ge(p_sq, B_sq + 8 * dqg)
                            half_off = (dqg % 2) * (CC * 512)
                            mm = None
                            for cc in range(CC):
                                for dq in range(4):
                                    mm = t.matmul(
                                        PQ[dq][:],
                                        wq_v[:, half_off + cc * 512 + dq * 128:
                                             half_off + cc * 512 + dq * 128 + 128],
                                        xn_buf[:, cc * 512:(cc + 1) * 512],
                                        start=(cc == 0), stop=(cc == CC - 1))
                            mm.then_inc(p_q, 1)
                        if step >= 1:
                            dqh = step - 1
                            for dq in range(4):
                                hh = dqh * 4 + dq
                                g = hh // 4
                                t.wait_ge(p_sq, B_sq + 2 * hh + 2)
                                if p > 0 or hh >= 1:
                                    t.wait_ge(p_hv, B_hv + 2 * hh)
                                t.matmul(pE[0:1, :], ksum_sb[:, g:g + 1],
                                         sq_buf[:, dq * 512:(dq + 1) * 512],
                                         start=True, stop=True).then_inc(p_hd, 1)
                                t.wait_ge(p_hv, B_hv + 2 * hh + 1)
                                if p > 0 or hh >= 1:
                                    t.wait_ge(p_he, B_he + hh)
                                t.matmul(pF[:], ones_row_f[0:1, :], rn_f32[0:1, :],
                                         start=True, stop=True).then_inc(p_hd, 1)
                                t.matmul(pE[:], kv_sb[:, g * 128:(g + 1) * 128],
                                         sq_buf[:, dq * 512:(dq + 1) * 512],
                                         start=True, stop=True).then_inc(p_hd, 1)
                    t.wait_ge(p_hv, B_hv + 2 * HN)
                    for og in range(OGN):
                        mm = None
                        for hh in range(HN):
                            hq = hh // 4
                            idx = og * 4 + hq
                            t.wait_ge(p_wo, B_wo + 16 * (idx + 1))
                            for tt in range(TPP):
                                if hh == 0 and og >= 1:
                                    t.wait_ge(p_oa, B_oa + (og - 1) * TPP + tt + 1)
                                mm = t.matmul(PQ[tt][:],
                                              attn_fm[:, hh * 512 + tt * 128:hh * 512 + tt * 128 + 128],
                                              wo_buf[idx % 2][:, (hh % 4) * 512:(hh % 4 + 1) * 512],
                                              start=(hh == 0), stop=(hh == HN - 1))
                            if hh % 4 == 3:
                                mm.then_inc(p_wu, 1)
                    for hc in range(ICN):
                        psg = pA if hc % 2 == 0 else pC
                        psu = pB if hc % 2 == 0 else pD
                        if hc == 0:
                            t.wait_ge(p_oa, B_oa + OGN * TPP)
                        if hc >= 2:
                            t.wait_ge(p_hh, B_hh + hc - 1)
                        for half, ps in ((0, psg), (1, psu)):
                            gi = 2 * hc + half
                            t.wait_ge(p_gl, B_gl + 16 * (gi + 1))
                            mm = None
                            for cc in range(CC):
                                mm = t.matmul(ps[:],
                                              gu_buf[gi % 3][:, cc * 128:(cc + 1) * 128],
                                              xn_buf[:, cc * 512:(cc + 1) * 512],
                                              start=(cc == 0), stop=(cc == CC - 1))
                            mm.then_inc(p_gm, 1)
                    for og in range(OGN):
                        for icq in range(4):
                            idx = og * 4 + icq
                            t.wait_ge(p_dl, B_dl + 32 * (idx + 1))
                            if og == 0:
                                t.wait_ge(p_hh, B_hh + icq * 11 + 11)
                            if icq == 0 and og >= 1:
                                t.wait_ge(p_ye, B_ye + og * TPP)
                            mm = None
                            for tt in range(TPP):
                                for ic in range(11):
                                    icg = icq * 11 + ic
                                    mm = t.matmul(PQ[tt][:],
                                                  h_sb[:, icg * 512 + tt * 128:icg * 512 + tt * 128 + 128],
                                                  dn_buf[idx % 2][:, ic * 512:(ic + 1) * 512],
                                                  start=(icq == 0 and ic == 0),
                                                  stop=(icq == 3 and ic == 10))
                            mm.then_inc(p_dm, 1)

                @blk.vector
                def _(v, B_q=B_q, B_ex=B_ex, B_hd=B_hd, B_he=B_he, B_wu=B_wu,
                      B_xa=B_xa, B_gm=B_gm, B_sl=B_sl, B_dm=B_dm, B_rs=B_rs,
                      B_oa=B_oa, B_r1=B_r1, B_r2=B_r2):
                    if B_wu > 0:
                        v.wait_ge(p_wu, B_wu)
                    for dqg in range(4):
                        v.wait_ge(p_q, B_q + dqg + 1)
                        for dq in range(4):
                            ti = dqg * 4 + dq
                            v.tensor_scalar(elu_a[:], PQ[dq][:], bq_sb[:, ti:ti + 1], 0.0,
                                            ALU.add, ALU.min).then_inc(p_sq, 1)
                            v.wait_ge(p_ex, B_ex + ti + 1)
                            if dqg >= 1:
                                v.wait_ge(p_hd, B_hd + 3 * ((dqg - 1) * 4 + dq + 1))
                            v.scalar_tensor_tensor(sq_buf[:, dq * 512:(dq + 1) * 512],
                                                   PQ[dq][:], bq1_sb[:, ti:ti + 1], elu_b[:],
                                                   ALU.add, ALU.max).then_inc(p_sq, 1)
                        for dq in range(4):
                            hh = dqg * 4 + dq
                            v.wait_ge(p_hd, B_hd + 3 * hh + 1)
                            inst = v.tensor_scalar(rn_f32[0:1, :], pE[0:1, :], 1e-6, None, ALU.add)
                            inst.then_inc(pc_v, 1); pcv[0] += 1; v.wait_ge(pc_v, pcv[0])
                            inst = v.reciprocal(rn_f32[0:1, :], rn_f32[0:1, :])
                            inst.then_inc(pc_v, 1); pcv[0] += 1; v.wait_ge(pc_v, pcv[0])
                            v.tensor_copy(rn_f32[0:1, 0:1], rn_f32[0:1, 0:1]).then_inc(p_hv, 1)
                            v.wait_ge(p_hd, B_hd + 3 * hh + 3)
                            v.wait_ge(p_he, B_he + hh + 1)
                            v.tensor_tensor(attn_fm[:, hh * 512:(hh + 1) * 512],
                                            pE[:], rn_bc[:], ALU.mult).then_inc(p_hv, 1)
                    for og in range(OGN):
                        v.wait_ge(p_wu, B_wu + 4 * (og + 1))
                        for tt in range(TPP):
                            v.wait_ge(p_xa, B_xa + 16 * (tt + 1))
                            v.tensor_tensor(out_acc[tt][:, og * 512:(og + 1) * 512],
                                            out_acc[tt][:, og * 512:(og + 1) * 512],
                                            PQ[tt][:], ALU.add).then_inc(p_oa, 1)
                    if B_dm > 0:
                        v.wait_ge(p_dm, B_dm)
                    for hc in range(ICN):
                        psg = pA if hc % 2 == 0 else pC
                        psu = pB if hc % 2 == 0 else pD
                        v.wait_ge(p_gm, B_gm + 2 * hc + 2)
                        v.wait_ge(p_sl, B_sl + hc + 1)
                        if dbg and p == 0 and hc == 0:
                            v.tensor_copy(u0[:, 0:512], psg[:])
                            v.tensor_copy(u0[:, 512:1024], psu[:])
                            v.tensor_copy(u0[:, 1024:1536], sg_buf[hc % 2][:])
                        v.tensor_tensor(h_sb[:, hc * 512:(hc + 1) * 512],
                                        sg_buf[hc % 2][:], psu[:], ALU.mult).then_inc(p_hh, 1)
                    for og in range(OGN):
                        if dbg and p == 0 and og == 0:
                            v.wait_ge(p_dbg, 32)
                        v.wait_ge(p_dm, B_dm + 4 * (og + 1))
                        for tt in range(TPP):
                            v.tensor_copy(u0[:, tt * D + og * 512:tt * D + (og + 1) * 512],
                                          PQ[tt][:]).then_inc(p_ye, 1)
                    for tt in range(TPP):
                        v.wait_ge(p_r1, B_r1 + tt + 1)
                        yjunk = dn_buf[0].ap().bitcast(F32)
                        inst = v.reduce_sum(s2_col[:, 0:1], yjunk[:, 0:D], axis=AXL.X)
                        inst.then_inc(pc_v, 1); pcv[0] += 1; v.wait_ge(pc_v, pcv[0])
                        v.tensor_copy(s2_col[0:1, 0:1], s2_col[0:1, 0:1]).then_inc(p_r2, 1)
                        v.wait_ge(p_rs, B_rs + tt + 1)
                        v.wait_ge(p_oa, B_oa + 3 * TPP + tt + 1)
                        v.scalar_tensor_tensor(out_acc[tt][:], u0[:, tt * D:(tt + 1) * D],
                                               r2_col[:, 0:1], out_acc[tt][:],
                                               ALU.mult, ALU.add).then_inc(p_fin, 1)

    return nc


def kernel(L0_attn_hidden_states, L0_attn_Wq, L0_attn_bq, L0_attn_Wk, L0_attn_bk,
           L0_attn_Wv, L0_attn_bv, L0_attn_Wo, L0_ffn_gate_up_packed,
           L0_ffn_gate_up_scales, L0_ffn_down_packed, L0_ffn_down_scales):
    x = np.ascontiguousarray(np.asarray(L0_attn_hidden_states, dtype=np.float32))
    wq_t = np.ascontiguousarray(np.asarray(L0_attn_Wq, dtype=np.float32).T)
    wk_t = np.ascontiguousarray(np.asarray(L0_attn_Wk, dtype=np.float32).T)
    wv_t = np.ascontiguousarray(np.asarray(L0_attn_Wv, dtype=np.float32).T)
    wo_t = np.ascontiguousarray(np.asarray(L0_attn_Wo, dtype=np.float32).T)
    bq = np.ascontiguousarray(np.asarray(L0_attn_bq, dtype=np.float32))
    bk = np.ascontiguousarray(np.asarray(L0_attn_bk, dtype=np.float32))
    bv = np.ascontiguousarray(np.asarray(L0_attn_bv, dtype=np.float32))
    gup = np.ascontiguousarray(np.asarray(L0_ffn_gate_up_packed, dtype=np.int32)).view(np.uint8)
    dnp = np.ascontiguousarray(np.asarray(L0_ffn_down_packed, dtype=np.int32)).view(np.uint8)
    gus = np.ascontiguousarray(np.asarray(L0_ffn_gate_up_scales, dtype=np.float16))
    dns = np.ascontiguousarray(np.asarray(L0_ffn_down_scales, dtype=np.float16))
    idb = np.eye(128, dtype=np.float32).astype(ml_dtypes.bfloat16)
    idh = np.eye(128, dtype=np.float32).astype(np.float16)

    shared = {"wq_t": wq_t, "wk_t": wk_t, "wv_t": wv_t, "wo_t": wo_t,
              "bq": bq, "bk": bk, "bv": bv, "idb": idb, "idh": idh}
    in_maps = []
    for c in range(8):
        in_maps.append({
            **shared,
            "x": np.ascontiguousarray(x[c]),
            "gup": np.ascontiguousarray(gup[c * 11 * 128:(c + 1) * 11 * 128, :]),
            "dnp": np.ascontiguousarray(dnp[c * 2 * 128:(c + 1) * 2 * 128, :]),
            "gus": np.ascontiguousarray(gus[:, c * 11 * 128:(c + 1) * 11 * 128]),
            "dns": np.ascontiguousarray(dns[:, c * 2 * 128:(c + 1) * 2 * 128]),
        })

    nc = build_program()
    res = run_bass_kernel_spmd(nc, in_maps, core_ids=list(range(8)))
    return np.stack([res.results[c]["out"] for c in range(8)], axis=0)


# version 15 (final)
# speedup vs baseline: 1.0589x; 1.0589x over previous
import numpy as np
import ml_dtypes
import concourse.bass as bass
import concourse.mybir as mybir
from concourse.bass_utils import run_bass_kernel_spmd
from contextlib import ExitStack

F32 = mybir.dt.float32
BF16 = mybir.dt.bfloat16
FP16 = mybir.dt.float16
I16 = mybir.dt.int16
U8 = mybir.dt.uint8
AF = mybir.ActivationFunctionType
AXL = mybir.AxisListType
ALU = mybir.AluOpType

S = 2048
D = 2048
HN = 16
KVH = 4
I_FF = 5632
GU = 2 * I_FF
CC = 16
TTN = 16
NPASS = 4
TPP = 4
TPASS = 512
ICN = 44
OGN = 4
JT_GU = 88
OT_DN = 16


def build_program(dbg=False, upto=None):
    nc = bass.Bass("TRN2", target_bir_lowering=False, detect_race_conditions=False)

    x_d = nc.dram_tensor("x", [S, D], F32, kind="ExternalInput").ap()
    wq_d = nc.dram_tensor("wq_t", [D, D], F32, kind="ExternalInput").ap()
    wk_d = nc.dram_tensor("wk_t", [D, 512], F32, kind="ExternalInput").ap()
    wv_d = nc.dram_tensor("wv_t", [D, 512], F32, kind="ExternalInput").ap()
    wo_d = nc.dram_tensor("wo_t", [D, D], F32, kind="ExternalInput").ap()
    bq_d = nc.dram_tensor("bq", [D], F32, kind="ExternalInput").ap()
    bk_d = nc.dram_tensor("bk", [512], F32, kind="ExternalInput").ap()
    bv_d = nc.dram_tensor("bv", [512], F32, kind="ExternalInput").ap()
    gup_d = nc.dram_tensor("gup", [11 * 128, D // 2], U8, kind="ExternalInput").ap()
    dnp_d = nc.dram_tensor("dnp", [2 * 128, I_FF // 2], U8, kind="ExternalInput").ap()
    gus_d = nc.dram_tensor("gus", [CC, 11 * 128], FP16, kind="ExternalInput").ap()
    dns_d = nc.dram_tensor("dns", [ICN, 2 * 128], FP16, kind="ExternalInput").ap()
    idb_d = nc.dram_tensor("idb", [128, 128], BF16, kind="ExternalInput").ap()
    idh_d = nc.dram_tensor("idh", [128, 128], FP16, kind="ExternalInput").ap()
    out_d = nc.dram_tensor("out", [S, D], F32, kind="ExternalOutput").ap()

    gu_loc = nc.dram_tensor("gu_loc", [CC * 128, 11 * 128], BF16).ap()
    dn_loc = nc.dram_tensor("dn_loc", [ICN * 128, 2 * 128], BF16).ap()
    gu_ag = nc.dram_tensor("gu_ag", [8 * CC * 128, 11 * 128], BF16, addr_space="Shared").ap()
    dn_ag = nc.dram_tensor("dn_ag", [8 * ICN * 128, 2 * 128], BF16, addr_space="Shared").ap()
    bar_in = nc.dram_tensor("bar_in", [1, 16], F32).ap()
    bar_out = nc.dram_tensor("bar_out", [1, 16], F32, addr_space="Shared").ap()
    xn_sc = nc.dram_tensor("xn_sc", [CC, 128, S], BF16).ap()

    if dbg:
        dbg_gu = nc.dram_tensor("dbg_gu", [8 * CC * 128, 11 * 128], BF16, kind="ExternalOutput").ap()
        dbg_dn = nc.dram_tensor("dbg_dn", [8 * ICN * 128, 2 * 128], BF16, kind="ExternalOutput").ap()
        dbg_xn = nc.dram_tensor("dbg_xn", [CC, 128, S], BF16, kind="ExternalOutput").ap()
        dbg_kv = nc.dram_tensor("dbg_kv", [128, 512], BF16, kind="ExternalOutput").ap()
        dbg_ks = nc.dram_tensor("dbg_ks", [128, 4], BF16, kind="ExternalOutput").ap()
        dbg_attn = nc.dram_tensor("dbg_attn", [128, HN * 512], BF16, kind="ExternalOutput").ap()
        dbg_h = nc.dram_tensor("dbg_h", [128, ICN * 512], BF16, kind="ExternalOutput").ap()
        dbg_y = nc.dram_tensor("dbg_y", [128, TPP * D], F32, kind="ExternalOutput").ap()
        dbg_st = nc.dram_tensor("dbg_st", [128, 96], F32, kind="ExternalOutput").ap()
        dbg_gup = nc.dram_tensor("dbg_gup", [128, 1536], F32, kind="ExternalOutput").ap()
        dbg_xnt = nc.dram_tensor("dbg_xnt", [128, D], BF16, kind="ExternalOutput").ap()

    ctx = ExitStack()
    with ctx:
        sbuf = lambda n, sh, dt: ctx.enter_context(nc.sbuf_tensor(n, sh, dt))
        psum = lambda n, sh, dt: ctx.enter_context(nc.psum_tensor(n, sh, dt))
        sem = lambda n: ctx.enter_context(nc.semaphore(n))

        bq_sb = sbuf("bq_sb", [128, CC], F32)
        bq1_sb = sbuf("bq1_sb", [128, CC], F32)
        bkv_sb = sbuf("bkv_sb", [1, 1024], BF16)
        ones_col = sbuf("ones_col", [128, 1], BF16)
        ones_row = sbuf("ones_row", [1, 128], BF16)
        idb_sb = sbuf("idb_sb", [128, 128], BF16)
        kv_sb = sbuf("kv_sb", [128, 512], BF16)
        eps5_sb = sbuf("eps5_sb", [128, 1], F32)
        eps6_sb = sbuf("eps6_sb", [128, 1], F32)
        elu_a = sbuf("elu_a", [128, 512], F32)
        elu_b = sbuf("elu_b", [128, 512], F32)
        ksum_sb = sbuf("ksum_sb", [128, 4], BF16)

        pA = psum("pA", [128, 512], F32)
        pB = psum("pB", [128, 512], F32)
        pC = psum("pC", [128, 512], F32)
        pD = psum("pD", [128, 512], F32)
        pE = psum("pE", [128, 512], F32)
        pF = psum("pF", [128, 512], F32)
        pT = psum("pT", [128, 1024], BF16)
        pS = psum("pS", [128, 512], FP16)
        PQ = (pA, pB, pC, pD)

        with nc.Block("init", no_gpsimd_drain=True) as blk:
            i_s = sem("i_s")
            i_g = sem("i_g")
            i_m = sem("i_m")

            @blk.sync
            def _(sy):
                sy.dma_start(out=bq_sb[:], in_=bq_d.rearrange("(k p) -> p k", p=128),
                             allow_slow_non_contiguous=True).then_inc(i_s, 16)
                sy.dma_start(out=idb_sb[:], in_=idb_d).then_inc(i_s, 16)
                sy.wait_ge(i_s, 32)

            @blk.gpsimd
            def _(g):
                g.dma_start(out=bkv_sb[0:1, 0:512],
                            in_=bk_d.rearrange("(o f) -> o f", o=1)).then_inc(i_g, 16)
                g.dma_start(out=bkv_sb[0:1, 512:1024],
                            in_=bv_d.rearrange("(o f) -> o f", o=1)).then_inc(i_g, 16)
                g.memset(ones_col[:], 1.0)
                g.memset(eps5_sb[:], 1e-5)
                g.memset(eps6_sb[:], 1e-6)
                g.memset(ones_row[:], 1.0).then_inc(i_m, 1)
                g.wait_ge(i_g, 32)

            @blk.vector
            def _(v):
                v.wait_ge(i_s, 16)
                v.tensor_scalar(bq1_sb[:], bq_sb[:], 1.0, None, ALU.add)
                v.wait_ge(i_m, 1)

        cc_sem = sem("cc_sem")
        cc_dma = sem("cc_dma")
        cc_dma2 = sem("cc_dma2")
        if upto == "init":
            return nc

        tiles = [("gu", jt) for jt in range(11)] + [("dn", ot) for ot in range(2)]
        nbat = [2 if k == "gu" else 6 for k, _ in tiles]
        cumbat = [0]
        for nb_ in nbat:
            cumbat.append(cumbat[-1] + nb_)

        pctx = ExitStack()
        psb = lambda n, sh, dt: pctx.enter_context(nc.sbuf_tensor(n, sh, dt))
        idh_sb = psb("idh_sb", [128, 128], FP16)
        gus_sb = psb("gus_sb", [CC, 11 * 128], FP16)
        dns_sb = psb("dns_sb", [ICN, 2 * 128], FP16)
        MAXB = I_FF // 2
        byt = [psb(f"byt{i}", [128, MAXB], U8) for i in range(2)]
        v16 = psb("v16", [128, MAXB], I16)
        pl_lo = psb("pl_lo", [128, MAXB], I16)
        pl_hi = psb("pl_hi", [128, MAXB], I16)
        sgn = psb("sgn", [128, MAXB], I16)
        cmb = psb("cmb", [128, MAXB], I16)
        wdq = [psb(f"wdq{i}", [128, I_FF], BF16) for i in range(2)]
        wtp = [psb(f"wtp{i}", [128, I_FF], BF16) for i in range(2)]
        sc_col = psb("sc_col", [128, ICN], F32)
        x_t = [psb(f"x_t{i}", [128, D], F32) for i in range(2)]
        xn_tok = [psb(f"xn_tok{i}", [128, D], BF16) for i in range(2)]
        stage_x = [psb(f"stage_x{i}", [128, D], BF16) for i in range(2)]
        st_s1 = psb("st_s1", [128, 1], F32)
        st_s2 = psb("st_s2", [128, 1], F32)
        st_m = psb("st_m", [128, 1], F32)
        st_t2 = psb("st_t2", [128, 1], F32)
        st_var = psb("st_var", [128, 1], F32)
        st_ln = psb("st_ln", [128, 1], F32)
        st_rstd = psb("st_rstd", [128, 1], F32)
        st_nmr = psb("st_nmr", [128, 1], F32)
        ln_f32 = psb("ln_f32", [128, D], F32)
        xsq_sb = psb("xsq_sb", [128, D], F32)
        wk_sb = psb("wk_sb", [128, CC * 512], BF16)
        wv_sb = psb("wv_sb", [128, CC * 512], BF16)
        sk_tok = [psb(f"sk_tok{i}", [128, 512], BF16) for i in range(2)]
        v_tok = [psb(f"v_tok{i}", [128, 512], BF16) for i in range(2)]
        if dbg:
            dbgst_sb = psb("dbgst_sb", [128, 96], F32)

        pDQ = (pC.ap().bitcast(BF16), pD.ap().bitcast(BF16))

        with nc.Block("prep", no_gpsimd_drain=True) as blk:
            d_ld = sem("d_ld")
            d_sc = sem("d_sc")
            d_se = sem("d_se")
            d_dv = sem("d_dv")
            d_pe = sem("d_pe")
            d_ev = sem("d_ev")
            d_wr = sem("d_wr")
            l_xd = sem("l_xd")
            l_st = sem("l_st")
            l_vs = sem("l_vs")
            l_ls = sem("l_ls")
            l_xn = sem("l_xn")
            l_tr = sem("l_tr")
            l_ev = sem("l_ev")
            l_xw = sem("l_xw")
            l_wk = sem("l_wk")
            k_mm = sem("k_mm")
            k_ep = sem("k_ep")
            k_ee = sem("k_ee")
            k_kv = sem("k_kv")
            k_fin = sem("k_fin")
            c_v = sem("c_v")
            c_s = sem("c_s")
            cv = [0]
            cs = [0]

            @blk.sync
            def _(sy):
                sy.dma_start(out=gus_sb[:], in_=gus_d).then_inc(d_ld, 16)
                sy.dma_start(out=dns_sb[:], in_=dns_d).then_inc(d_ld, 16)
                sy.dma_start(out=idh_sb[:], in_=idh_d).then_inc(d_ld, 16)
                for s in range(TTN + 1):
                    if s < len(tiles):
                        kind, it = tiles[s]
                        nb = 1024 if kind == "gu" else 2816
                        srcb = (gup_d if kind == "gu" else dnp_d)[it * 128:(it + 1) * 128, :]
                        if s >= 2:
                            sy.wait_ge(d_dv, s - 1)
                        sy.dma_start(out=byt[s % 2][:, 0:nb], in_=srcb).then_inc(d_ld, 16)
                    if 1 <= s <= len(tiles):
                        j = s - 1
                        kindj, itj = tiles[j]
                        npc = 16 if kindj == "gu" else 44
                        sy.wait_ge(d_ev, cumbat[j + 1])
                        if kindj == "gu":
                            dst = gu_loc.rearrange("(cc c) j -> c cc j", c=128)[:, :, itj * 128:(itj + 1) * 128]
                        else:
                            dst = dn_loc.rearrange("(ic i) o -> i ic o", i=128)[:, :, itj * 128:(itj + 1) * 128]
                        sy.dma_start(out=dst, in_=wtp[j % 2][:, 0:npc * 128]).then_inc(d_wr, 16)
                    if s < TTN:
                        sy.wait_ge(l_ev, 2 * (s + 1))
                        dst = xn_sc.rearrange("cc c t -> c cc t")[:, :, s * 128:(s + 1) * 128]
                        sy.dma_start(out=dst, in_=stage_x[s % 2][:]).then_inc(l_xw, 16)
                sy.wait_ge(d_wr, 16 * len(tiles))
                sy.wait_ge(l_xw, 16 * TTN)

            @blk.gpsimd
            def _(g):
                for s in range(2):
                    g.dma_start(out=x_t[s][:],
                                in_=x_d[s * 128:(s + 1) * 128, :]).then_inc(l_xd, 16)
                for cc in range(CC):
                    g.dma_start(out=wk_sb[:, cc * 512:(cc + 1) * 512],
                                in_=wk_d[cc * 128:(cc + 1) * 128, :]).then_inc(l_wk, 16)
                    g.dma_start(out=wv_sb[:, cc * 512:(cc + 1) * 512],
                                in_=wv_d[cc * 128:(cc + 1) * 128, :]).then_inc(l_wk, 16)
                for s in range(2, TTN):
                    g.wait_ge(l_xn, s - 1)
                    g.dma_start(out=x_t[s % 2][:],
                                in_=x_d[s * 128:(s + 1) * 128, :]).then_inc(l_xd, 16)
                g.wait_ge(d_wr, 16 * len(tiles))
                g.collective_compute(
                    "AllGather", ALU.bypass, replica_groups=[list(range(8))],
                    ins=[gu_loc.opt()], outs=[gu_ag.opt()]).then_inc(cc_sem, 1)
                g.collective_compute(
                    "AllGather", ALU.bypass, replica_groups=[list(range(8))],
                    ins=[dn_loc.opt()], outs=[dn_ag.opt()]).then_inc(cc_sem, 1)
                g.memset(sc_col[0:1, 0:16], 0.0).then_inc(cc_dma, 1)
                g.wait_ge(cc_dma, 1)
                g.dma_start(out=bar_in, in_=sc_col[0:1, 0:16]).then_inc(cc_dma2, 16)
                g.wait_ge(cc_dma2, 16)
                g.collective_compute(
                    "AllReduce", ALU.add, replica_groups=[list(range(8))],
                    ins=[bar_in.opt()], outs=[bar_out.opt()]).then_inc(cc_sem, 1)

            @blk.scalar
            def _(s):
                for st in range(TTN):
                    s.wait_ge(l_xd, 16 * (st + 1))
                    s.activation(xsq_sb[:], x_t[st % 2][:], AF.Square).then_inc(l_st, 1)
                    s.wait_ge(l_vs, st + 1)
                    inst = s.activation(st_ln[:, 0:1], st_var[:, 0:1], AF.Ln, bias=eps5_sb[:, 0:1])
                    inst.then_inc(c_s, 1)
                    cs[0] += 1
                    s.wait_ge(c_s, cs[0])
                    s.activation(st_rstd[:, 0:1], st_ln[:, 0:1], AF.Exp,
                                 scale=-0.5).then_inc(l_ls, 1)
                    if st < len(tiles):
                        ng = 16 if tiles[st][0] == "gu" else 44
                        s.wait_ge(d_sc, st + 1)
                        s.mul(sc_col[:, 0:ng], pS[:, 0:ng], float(2.0 ** 14)).then_inc(d_se, 1)
                    if st >= 2:
                        s.wait_ge(l_xw, 16 * (st - 1))
                        s.wait_ge(k_mm, st - 1)
                    for b in range(2):
                        s.wait_ge(l_tr, 2 * st + b + 1)
                        s.copy(stage_x[st % 2][:, b * 1024:(b + 1) * 1024],
                               pT[:]).then_inc(l_ev, 1)
                    if st >= 1:
                        s.wait_ge(k_ep, 3 * (st - 1) + 1)
                        s.activation(elu_b[:], elu_a[:], AF.Exp).then_inc(k_ee, 1)
                    if 1 <= st <= len(tiles):
                        j = st - 1
                        ng = 16 if tiles[j][0] == "gu" else 44
                        if j >= 2:
                            s.wait_ge(d_wr, 16 * (j - 1))
                        for b in range(nbat[j]):
                            gb = cumbat[j] + b
                            s.wait_ge(d_pe, gb + 1)
                            lo, hi = b * 8, min(b * 8 + 8, ng)
                            s.copy(wtp[j % 2][:, lo * 128:hi * 128],
                                   pDQ[gb % 2][:, 0:(hi - lo) * 128]).then_inc(d_ev, 1)
                s.wait_ge(k_ep, 3 * (TTN - 1) + 1)
                s.activation(elu_b[:], elu_a[:], AF.Exp).then_inc(k_ee, 1)

            @blk.vector
            def _(v):
                def vs(inst):
                    inst.then_inc(c_v, 1)
                    cv[0] += 1
                    v.wait_ge(c_v, cv[0])
                for st in range(TTN):
                    v.wait_ge(l_st, st + 1)
                    vs(v.reduce_sum(st_s1[:, 0:1], x_t[st % 2][:], axis=AXL.X))
                    vs(v.reduce_sum(st_s2[:, 0:1], xsq_sb[:], axis=AXL.X))
                    vs(v.tensor_scalar(st_m[:, 0:1], st_s1[:, 0:1], 1.0 / D, None, ALU.mult))
                    vs(v.tensor_scalar(st_t2[:, 0:1], st_s2[:, 0:1], 1.0 / D, None, ALU.mult))
                    vs(v.tensor_tensor(st_var[:, 0:1], st_m[:, 0:1], st_m[:, 0:1], ALU.mult))
                    vs(v.tensor_tensor(st_var[:, 0:1], st_t2[:, 0:1], st_var[:, 0:1],
                                       ALU.subtract))
                    v.sem_inc(l_vs, 1)
                    v.wait_ge(l_ls, st + 1)
                    vs(v.tensor_scalar(st_nmr[:, 0:1], st_m[:, 0:1], -1.0, None, ALU.mult))
                    vs(v.tensor_tensor(st_nmr[:, 0:1], st_nmr[:, 0:1], st_rstd[:, 0:1],
                                       ALU.mult))
                    if st >= 2:
                        v.wait_ge(l_tr, 2 * (st - 1))
                    v.tensor_scalar(ln_f32[:], x_t[st % 2][:],
                                    st_nmr[:, 0:1], None, ALU.add)
                    v.tensor_scalar(xn_tok[st % 2][:], ln_f32[:],
                                    st_rstd[:, 0:1], None, ALU.mult)
                    if dbg:
                        v.tensor_copy(dbgst_sb[:, st:st + 1], st_s1[:, 0:1])
                        v.tensor_copy(dbgst_sb[:, 16 + st:17 + st], st_var[:, 0:1])
                        v.tensor_copy(dbgst_sb[:, 32 + st:33 + st], st_rstd[:, 0:1])
                    v.tensor_copy(st_s1[0:1, 0:1], st_s1[0:1, 0:1]).then_inc(l_xn, 1)
                    if st >= 1:
                        j = st - 1
                        v.wait_ge(k_mm, j + 1)
                        if j >= 2:
                            v.wait_ge(k_kv, j - 1)
                        v.tensor_scalar(elu_a[:], pA[:], 0.0, None, ALU.min).then_inc(k_ep, 1)
                        v.tensor_copy(v_tok[j % 2][:], pB[:]).then_inc(k_ep, 1)
                        v.wait_ge(k_ee, j + 1)
                        v.scalar_tensor_tensor(sk_tok[j % 2][:], pA[:], 1.0, elu_b[:],
                                               ALU.add, ALU.max).then_inc(k_ep, 1)
                    if st < len(tiles):
                        i = st
                        kind, it = tiles[i]
                        ng = 16 if kind == "gu" else 44
                        nb = 64 * ng
                        v.wait_ge(d_ld, 16 * (i + 4))
                        v.wait_ge(d_se, i + 1)
                        if i >= 2:
                            v.wait_ge(d_pe, cumbat[i - 1])
                        v.tensor_copy(v16[:, 0:nb], byt[i % 2][:, 0:nb])
                        v.tensor_scalar(pl_lo[:, 0:nb], v16[:, 0:nb], 15, 9,
                                        ALU.bitwise_and, ALU.logical_shift_left)
                        v.tensor_scalar(pl_hi[:, 0:nb], v16[:, 0:nb], 0xF0, 5,
                                        ALU.bitwise_and, ALU.logical_shift_left)
                        wint = wdq[i % 2].ap().rearrange("p (c two) -> p c two", two=2)
                        last = None
                        for plane, pl in ((0, pl_lo), (1, pl_hi)):
                            v.tensor_scalar(sgn[:, 0:nb], pl[:, 0:nb], 0x1000, 3,
                                            ALU.bitwise_and, ALU.logical_shift_left)
                            v.tensor_scalar(cmb[:, 0:nb], pl[:, 0:nb], 0x0E00, None, ALU.bitwise_and)
                            v.tensor_tensor(cmb[:, 0:nb], cmb[:, 0:nb], sgn[:, 0:nb], ALU.bitwise_or)
                            cview = cmb.ap().bitcast(FP16)
                            for gsl in range(ng):
                                last = v.tensor_scalar(
                                    wint[:, 64 * gsl:64 * (gsl + 1), plane:plane + 1],
                                    cview[:, 64 * gsl:64 * (gsl + 1)],
                                    sc_col[:, gsl:gsl + 1], None, ALU.mult)
                        last.then_inc(d_dv, 1)
                j = TTN - 1
                v.wait_ge(k_mm, j + 1)
                v.wait_ge(k_kv, j - 1)
                v.tensor_scalar(elu_a[:], pA[:], 0.0, None, ALU.min).then_inc(k_ep, 1)
                v.tensor_copy(v_tok[j % 2][:], pB[:]).then_inc(k_ep, 1)
                v.wait_ge(k_ee, j + 1)
                v.scalar_tensor_tensor(sk_tok[j % 2][:], pA[:], 1.0, elu_b[:],
                                       ALU.add, ALU.max).then_inc(k_ep, 1)
                v.wait_ge(k_kv, TTN)
                v.tensor_copy(kv_sb[:], pE[:])
                v.tensor_copy(ksum_sb[:], pF[:, 0:4]).then_inc(k_fin, 1)

            @blk.tensor
            def _(t):
                t.wait_ge(d_ld, 48)
                for st in range(TTN):
                    if st < len(tiles):
                        ng = 16 if tiles[st][0] == "gu" else 44
                        if st >= 1:
                            t.wait_ge(d_se, st)
                        ssrc = (gus_sb if tiles[st][0] == "gu" else dns_sb)[
                            0:ng, tiles[st][1] * 128:(tiles[st][1] + 1) * 128]
                        t.transpose(pS[:, 0:ng], ssrc, idh_sb[0:ng, 0:ng]).then_inc(d_sc, 1)
                    t.wait_ge(l_xn, st + 1)
                    for b in range(2):
                        gb = 2 * st + b
                        if gb >= 1:
                            t.wait_ge(l_ev, gb)
                        mm = None
                        for k in range(8):
                            cc = b * 8 + k
                            mm = t.transpose(pT[:, k * 128:(k + 1) * 128],
                                             xn_tok[st % 2][:, cc * 128:(cc + 1) * 128],
                                             idb_sb[:])
                        mm.then_inc(l_tr, 1)
                    t.wait_ge(l_ev, 2 * (st + 1))
                    if st == 0:
                        t.wait_ge(l_wk, 16 * 2 * CC)
                    if st >= 1:
                        t.wait_ge(k_ep, 3 * st)
                    for cc in range(CC):
                        t.matmul(pA[:], stage_x[st % 2][:, cc * 128:(cc + 1) * 128],
                                 wk_sb[:, cc * 512:(cc + 1) * 512], start=(cc == 0), stop=False)
                    t.matmul(pA[:], ones_row[0:1, :], bkv_sb[0:1, 0:512],
                             start=False, stop=True)
                    for cc in range(CC):
                        t.matmul(pB[:], stage_x[st % 2][:, cc * 128:(cc + 1) * 128],
                                 wv_sb[:, cc * 512:(cc + 1) * 512], start=(cc == 0), stop=False)
                    t.matmul(pB[:], ones_row[0:1, :], bkv_sb[0:1, 512:1024],
                             start=False, stop=True).then_inc(k_mm, 1)
                    if st >= 1:
                        j = st - 1
                        t.wait_ge(k_ep, 3 * (j + 1))
                        mm = None
                        for gq in range(KVH):
                            t.matmul(pE[:, gq * 128:(gq + 1) * 128],
                                     sk_tok[j % 2][:, gq * 128:(gq + 1) * 128],
                                     v_tok[j % 2][:, gq * 128:(gq + 1) * 128],
                                     start=(j == 0 and gq == 0), stop=False,
                                     skip_group_check=True)
                            mm = t.matmul(pF[:, gq:gq + 1],
                                          sk_tok[j % 2][:, gq * 128:(gq + 1) * 128],
                                          ones_col[:, 0:1],
                                          start=(j == 0 and gq == 0), stop=False,
                                          skip_group_check=True)
                        mm.then_inc(k_kv, 1)
                    if 1 <= st <= len(tiles):
                        j = st - 1
                        ng = 16 if tiles[j][0] == "gu" else 44
                        t.wait_ge(d_dv, j + 1)
                        for b in range(nbat[j]):
                            gb = cumbat[j] + b
                            if gb >= 2:
                                t.wait_ge(d_ev, gb - 1)
                            lo, hi = b * 8, min(b * 8 + 8, ng)
                            mm = None
                            for k in range(lo, hi):
                                mm = t.transpose(pDQ[gb % 2][:, (k - lo) * 128:(k - lo + 1) * 128],
                                                 wdq[j % 2][:, k * 128:(k + 1) * 128], idb_sb[:])
                            mm.then_inc(d_pe, 1)
                j = TTN - 1
                t.wait_ge(k_ep, 3 * (j + 1))
                mm = None
                for gq in range(KVH):
                    t.matmul(pE[:, gq * 128:(gq + 1) * 128],
                             sk_tok[j % 2][:, gq * 128:(gq + 1) * 128],
                             v_tok[j % 2][:, gq * 128:(gq + 1) * 128],
                             start=False, stop=True, skip_group_check=True)
                    mm = t.matmul(pF[:, gq:gq + 1],
                                  sk_tok[j % 2][:, gq * 128:(gq + 1) * 128],
                                  ones_col[:, 0:1],
                                  start=False, stop=True, skip_group_check=True)
                mm.then_inc(k_kv, 1)

            if dbg:
                p_dbg0 = sem("p_dbg0")

                @blk.scalar
                def _(s):
                    s.wait_ge(cc_sem, 3)
                    s.dma_start(out=dbg_gu, in_=gu_ag).then_inc(p_dbg0, 16)
                    s.dma_start(out=dbg_dn, in_=dn_ag).then_inc(p_dbg0, 16)
                    s.wait_ge(l_xw, 16 * TTN)
                    s.dma_start(out=dbg_xn, in_=xn_sc).then_inc(p_dbg0, 16)
                    s.dma_start(out=dbg_st, in_=dbgst_sb[:]).then_inc(p_dbg0, 16)
                    s.wait_ge(p_dbg0, 64)

                @blk.gpsimd
                def _(g):
                    g.wait_ge(k_fin, 1)
                    g.dma_start(out=dbg_kv, in_=kv_sb[:]).then_inc(p_dbg0, 16)
                    g.dma_start(out=dbg_ks, in_=ksum_sb[:]).then_inc(p_dbg0, 16)
                    g.wait_ge(p_dbg0, 96)

        pctx.close()
        if upto in ("deq", "ln", "kv", "prep"):
            return nc

        xn_buf = sbuf("xn_buf", [128, CC * 512], BF16)
        out_acc = [sbuf(f"oacc{i}", [128, D], F32) for i in range(TPP)]
        h_sb = sbuf("h_sb", [128, ICN * 512], BF16)
        u0 = sbuf("u0", [128, TPP * D], F32)
        wq_v = u0.ap().bitcast(BF16)
        gu_buf = [sbuf(f"gu_buf{i}", [128, CC * 128], BF16) for i in range(3)]
        dn_buf = [sbuf(f"dn_buf{i}", [128, 11 * 512], BF16) for i in range(2)]
        wo_buf = [sbuf(f"wo_buf{i}", [128, 4 * 512], BF16) for i in range(2)]
        attn_fm = sbuf("attn_fm", [128, HN * 512], BF16)
        sq_buf = sbuf("sq_buf", [128, 4 * 512], BF16)
        rn_f32 = sbuf("rn_f32", [1, 512], F32)
        rn_bf = sbuf("rn_bf", [1, 512], BF16)
        rn_bc = sbuf("rn_bc", [128, 512], BF16)
        sg_buf = [sbuf(f"sg_buf{i}", [128, 512], F32) for i in range(2)]
        s2_col = sbuf("s2_col", [128, 1], F32)
        r2_ln = sbuf("r2_ln", [128, 1], F32)
        r2_col = sbuf("r2_col", [128, 1], F32)

        p_xn = sem("p_xn"); T_xn = 16 * CC
        p_wq = sem("p_wq"); T_wq = 16 * 4
        p_q = sem("p_q"); T_q = 4
        p_sq = sem("p_sq"); T_sq = 32
        p_ex = sem("p_ex"); T_ex = 16
        p_hd = sem("p_hd"); T_hd = 48
        p_hv = sem("p_hv"); T_hv = 32
        p_he = sem("p_he"); T_he = 16
        p_xa = sem("p_xa"); T_xa = 16 * TPP
        p_wo = sem("p_wo"); T_wo = 16 * OGN * 4
        p_wu = sem("p_wu"); T_wu = OGN * 4
        p_oa = sem("p_oa"); T_oa = OGN * TPP
        p_gl = sem("p_gl"); T_gl = 16 * 2 * ICN
        p_gm = sem("p_gm"); T_gm = 2 * ICN
        p_sl = sem("p_sl"); T_sl = ICN
        p_hh = sem("p_hh"); T_hh = ICN
        p_dl = sem("p_dl"); T_dl = 32 * 16
        p_dm = sem("p_dm"); T_dm = 16
        p_ye = sem("p_ye"); T_ye = OGN * TPP
        p_rs = sem("p_rs"); T_rs = TPP
        p_r1 = sem("p_r1"); T_r1 = TPP
        p_r2 = sem("p_r2"); T_r2 = TPP
        pc_v = sem("pc_v")
        pc_s = sem("pc_s")
        pcv = [0]
        pcs = [0]
        p_fin = sem("p_fin"); T_fin = TPP
        p_out = sem("p_out"); T_out = 16 * TPP
        p_dbg = sem("p_dbg")

        npass_build = NPASS
        if upto and upto.startswith("pass"):
            npass_build = int(upto[4:]) + 1
        for p in range(npass_build):
            B_xn = p * T_xn; B_wq = p * T_wq; B_q = p * T_q; B_sq = p * T_sq
            B_ex = p * T_ex; B_hd = p * T_hd; B_hv = p * T_hv; B_he = p * T_he
            B_xa = p * T_xa; B_wo = p * T_wo; B_wu = p * T_wu
            B_oa = p * T_oa; B_gl = p * T_gl; B_gm = p * T_gm; B_sl = p * T_sl
            B_hh = p * T_hh; B_dl = p * T_dl; B_dm = p * T_dm; B_ye = p * T_ye
            B_rs = p * T_rs; B_fin = p * T_fin; B_out = p * T_out
            B_r1 = p * T_r1; B_r2 = p * T_r2
            tok0 = p * TPASS

            with nc.Block(f"pass{p}", no_gpsimd_drain=True) as blk:

                @blk.sync
                def _(sy, tok0=tok0, B_gm=B_gm, B_dm=B_dm, B_r1=B_r1, B_r2=B_r2, p=p):
                    if p > 0:
                        sy.wait_ge(p_gm, B_gm)
                    for cc in range(CC):
                        sy.dma_start(out=xn_buf[:, cc * 512:(cc + 1) * 512],
                                     in_=xn_sc[cc, :, tok0:tok0 + TPASS]).then_inc(p_xn, 16)
                    if p == 0:
                        sy.wait_ge(cc_sem, 3)
                    for hc in range(ICN):
                        for half in range(2):
                            gi = 2 * hc + half
                            jt = hc + half * ICN
                            rk, jrt = jt // 11, jt % 11
                            if gi >= 3:
                                sy.wait_ge(p_gm, B_gm + gi - 2)
                            src = gu_ag[rk * CC * 128:(rk + 1) * CC * 128, :].rearrange(
                                "(cc c) j -> c cc j", c=128)[:, :, jrt * 128:(jrt + 1) * 128]
                            sy.dma_start(out=gu_buf[gi % 3][:], in_=src).then_inc(p_gl, 16)
                    for og in range(OGN):
                        for icq in range(4):
                            idx = og * 4 + icq
                            if idx >= 2:
                                sy.wait_ge(p_dm, B_dm + idx - 1)
                            elif p > 0:
                                sy.wait_ge(p_r1, B_r1)
                                sy.wait_ge(p_r2, B_r2)
                            for rr in range(2):
                                srcr = dn_ag.rearrange("(r ic i) o -> i ic r o", ic=ICN, i=128)[
                                    :, icq * 11:(icq + 1) * 11, 2 * og + rr, :]
                                dstv = dn_buf[idx % 2].ap().rearrange(
                                    "p (ic r o) -> p ic r o", ic=11, r=2, o=256)[:, :, rr, :]
                                sy.dma_start(out=dstv, in_=srcr).then_inc(p_dl, 16)

                @blk.scalar
                def _(s, tok0=tok0, B_sq=B_sq, B_hd=B_hd, B_gm=B_gm, B_ye=B_ye,
                      B_fin=B_fin, B_out=B_out, B_r2=B_r2, p=p):
                    for tt in range(TPP):
                        row = tok0 + tt * 128
                        s.dma_start(out=out_acc[tt][:],
                                    in_=x_d[row:row + 128, :]).then_inc(p_xa, 16)
                    for dqg in range(4):
                        for dq in range(4):
                            ti = dqg * 4 + dq
                            s.wait_ge(p_sq, B_sq + 2 * ti + 1)
                            s.activation(elu_b[:], elu_a[:], AF.Exp).then_inc(p_ex, 1)
                        for dq in range(4):
                            hh = dqg * 4 + dq
                            s.wait_ge(p_hd, B_hd + 3 * hh + 2)
                            s.copy(rn_bc[:], pF[:]).then_inc(p_he, 1)
                    for hc in range(ICN):
                        s.wait_ge(p_gm, B_gm + 2 * hc + 1)
                        psg = pA if hc % 2 == 0 else pC
                        s.activation(sg_buf[hc % 2][:], psg[:], AF.Silu).then_inc(p_sl, 1)
                    for tt in range(TPP):
                        s.wait_ge(p_ye, B_ye + 3 * TPP + tt + 1)
                        if p > 0 or tt >= 1:
                            s.wait_ge(p_fin, B_fin + tt)
                        yjunk = dn_buf[0].ap().bitcast(F32)
                        s.activation(yjunk[:, 0:D], u0[:, tt * D:(tt + 1) * D],
                                     AF.Square).then_inc(p_r1, 1)
                        s.wait_ge(p_r2, B_r2 + tt + 1)
                        inst = s.activation(r2_ln[:, 0:1], s2_col[:, 0:1], AF.Ln,
                                            scale=1.0 / D, bias=eps6_sb[:, 0:1])
                        inst.then_inc(pc_s, 1); pcs[0] += 1; s.wait_ge(pc_s, pcs[0])
                        s.activation(r2_col[:, 0:1], r2_ln[:, 0:1], AF.Exp,
                                     scale=-0.5).then_inc(p_rs, 1)
                    for tt in range(TPP):
                        s.wait_ge(p_fin, B_fin + tt + 1)
                        row = tok0 + tt * 128
                        s.dma_start(out=out_d[row:row + 128, :],
                                    in_=out_acc[tt][:]).then_inc(p_out, 16)
                    s.wait_ge(p_out, B_out + 16 * TPP)

                @blk.gpsimd
                def _(g, p=p, B_q=B_q, B_wu=B_wu, B_hv=B_hv, B_hh=B_hh, B_ye=B_ye):
                    if p > 0:
                        g.wait_ge(p_fin, B_fin)
                    for dqg in range(4):
                        if dqg >= 2:
                            g.wait_ge(p_q, B_q + dqg - 1)
                        half_off = (dqg % 2) * (CC * 512)
                        dstv = wq_v[:, half_off:half_off + CC * 512].rearrange(
                            "p (cc f) -> p cc f", cc=CC)
                        srcv = wq_d.rearrange("(cc c) q -> c cc q", c=128)[
                            :, :, dqg * 512:(dqg + 1) * 512]
                        g.dma_start(out=dstv, in_=srcv).then_inc(p_wq, 16)
                    for og in range(OGN):
                        for hq in range(4):
                            idx = og * 4 + hq
                            if idx >= 2:
                                g.wait_ge(p_wu, B_wu + idx - 1)
                            elif p > 0:
                                g.wait_ge(p_wu, B_wu + idx - 1)
                            dstv = wo_buf[idx % 2].ap().rearrange("p (hh f) -> p hh f", hh=4)
                            srcv = wo_d.rearrange("(hh h) q -> h hh q", h=128)[
                                :, hq * 4:(hq + 1) * 4, og * 512:(og + 1) * 512]
                            g.dma_start(out=dstv, in_=srcv).then_inc(p_wo, 16)
                    if dbg and p == 0:
                        g.wait_ge(p_hv, B_hv + 2 * HN)
                        g.dma_start(out=dbg_attn, in_=attn_fm[:]).then_inc(p_dbg, 16)
                        g.wait_ge(p_hh, B_hh + 1)
                        g.dma_start(out=dbg_gup, in_=u0[:, 0:1536]).then_inc(p_dbg, 16)
                        g.wait_ge(p_hh, B_hh + ICN)
                        g.dma_start(out=dbg_h, in_=h_sb[:]).then_inc(p_dbg, 16)
                        g.wait_ge(p_ye, B_ye + OGN * TPP)
                        g.dma_start(out=dbg_y, in_=u0[:]).then_inc(p_dbg, 16)
                        g.wait_ge(p_dbg, 64)

                @blk.tensor
                def _(t, B_xn=B_xn, B_wq=B_wq, B_sq=B_sq, B_hv=B_hv, B_he=B_he,
                      B_wo=B_wo, B_oa=B_oa, B_hh=B_hh, B_gl=B_gl, B_dl=B_dl,
                      B_ye=B_ye, p=p):
                    t.wait_ge(p_xn, B_xn + 16 * CC)
                    for step in range(5):
                        if step < 4:
                            dqg = step
                            t.wait_ge(p_wq, B_wq + 16 * (dqg + 1))
                            if dqg >= 1:
                                t.wait_ge(p_sq, B_sq + 8 * dqg)
                            half_off = (dqg % 2) * (CC * 512)
                            mm = None
                            for cc in range(CC):
                                for dq in range(4):
                                    mm = t.matmul(
                                        PQ[dq][:],
                                        wq_v[:, half_off + cc * 512 + dq * 128:
                                             half_off + cc * 512 + dq * 128 + 128],
                                        xn_buf[:, cc * 512:(cc + 1) * 512],
                                        start=(cc == 0), stop=(cc == CC - 1))
                            mm.then_inc(p_q, 1)
                        if step >= 1:
                            dqh = step - 1
                            for dq in range(4):
                                hh = dqh * 4 + dq
                                g = hh // 4
                                t.wait_ge(p_sq, B_sq + 2 * hh + 2)
                                if p > 0 or hh >= 1:
                                    t.wait_ge(p_hv, B_hv + 2 * hh)
                                t.matmul(pE[0:1, :], ksum_sb[:, g:g + 1],
                                         sq_buf[:, dq * 512:(dq + 1) * 512],
                                         start=True, stop=True).then_inc(p_hd, 1)
                                t.wait_ge(p_hv, B_hv + 2 * hh + 1)
                                if p > 0 or hh >= 1:
                                    t.wait_ge(p_he, B_he + hh)
                                t.matmul(pF[:], ones_row[0:1, :], rn_bf[0:1, :],
                                         start=True, stop=True).then_inc(p_hd, 1)
                                t.matmul(pE[:], kv_sb[:, g * 128:(g + 1) * 128],
                                         sq_buf[:, dq * 512:(dq + 1) * 512],
                                         start=True, stop=True).then_inc(p_hd, 1)
                    t.wait_ge(p_hv, B_hv + 2 * HN)
                    for og in range(OGN):
                        mm = None
                        for hh in range(HN):
                            hq = hh // 4
                            idx = og * 4 + hq
                            t.wait_ge(p_wo, B_wo + 16 * (idx + 1))
                            for tt in range(TPP):
                                if hh == 0 and og >= 1:
                                    t.wait_ge(p_oa, B_oa + (og - 1) * TPP + tt + 1)
                                mm = t.matmul(PQ[tt][:],
                                              attn_fm[:, hh * 512 + tt * 128:hh * 512 + tt * 128 + 128],
                                              wo_buf[idx % 2][:, (hh % 4) * 512:(hh % 4 + 1) * 512],
                                              start=(hh == 0), stop=(hh == HN - 1))
                            if hh % 4 == 3:
                                mm.then_inc(p_wu, 1)
                    for hc in range(ICN):
                        psg = pA if hc % 2 == 0 else pC
                        psu = pB if hc % 2 == 0 else pD
                        if hc == 0:
                            t.wait_ge(p_oa, B_oa + OGN * TPP)
                        if hc >= 2:
                            t.wait_ge(p_hh, B_hh + hc - 1)
                        for half, ps in ((0, psg), (1, psu)):
                            gi = 2 * hc + half
                            t.wait_ge(p_gl, B_gl + 16 * (gi + 1))
                            mm = None
                            for cc in range(CC):
                                mm = t.matmul(ps[:],
                                              gu_buf[gi % 3][:, cc * 128:(cc + 1) * 128],
                                              xn_buf[:, cc * 512:(cc + 1) * 512],
                                              start=(cc == 0), stop=(cc == CC - 1))
                            mm.then_inc(p_gm, 1)
                    for og in range(OGN):
                        for icq in range(4):
                            idx = og * 4 + icq
                            t.wait_ge(p_dl, B_dl + 32 * (idx + 1))
                            if og == 0:
                                t.wait_ge(p_hh, B_hh + icq * 11 + 11)
                            if icq == 0 and og >= 1:
                                t.wait_ge(p_ye, B_ye + og * TPP)
                            mm = None
                            for tt in range(TPP):
                                for ic in range(11):
                                    icg = icq * 11 + ic
                                    mm = t.matmul(PQ[tt][:],
                                                  h_sb[:, icg * 512 + tt * 128:icg * 512 + tt * 128 + 128],
                                                  dn_buf[idx % 2][:, ic * 512:(ic + 1) * 512],
                                                  start=(icq == 0 and ic == 0),
                                                  stop=(icq == 3 and ic == 10))
                            mm.then_inc(p_dm, 1)

                @blk.vector
                def _(v, B_q=B_q, B_ex=B_ex, B_hd=B_hd, B_he=B_he, B_wu=B_wu,
                      B_xa=B_xa, B_gm=B_gm, B_sl=B_sl, B_dm=B_dm, B_rs=B_rs,
                      B_oa=B_oa, B_r1=B_r1, B_r2=B_r2):
                    if B_wu > 0:
                        v.wait_ge(p_wu, B_wu)
                    for dqg in range(4):
                        v.wait_ge(p_q, B_q + dqg + 1)
                        for dq in range(4):
                            ti = dqg * 4 + dq
                            v.tensor_scalar(elu_a[:], PQ[dq][:], bq_sb[:, ti:ti + 1], 0.0,
                                            ALU.add, ALU.min).then_inc(p_sq, 1)
                            v.wait_ge(p_ex, B_ex + ti + 1)
                            if dqg >= 1:
                                v.wait_ge(p_hd, B_hd + 3 * ((dqg - 1) * 4 + dq + 1))
                            v.scalar_tensor_tensor(sq_buf[:, dq * 512:(dq + 1) * 512],
                                                   PQ[dq][:], bq1_sb[:, ti:ti + 1], elu_b[:],
                                                   ALU.add, ALU.max).then_inc(p_sq, 1)
                        for dq in range(4):
                            hh = dqg * 4 + dq
                            v.wait_ge(p_hd, B_hd + 3 * hh + 1)
                            inst = v.tensor_scalar(rn_f32[0:1, :], pE[0:1, :], 1e-6, None, ALU.add)
                            inst.then_inc(pc_v, 1); pcv[0] += 1; v.wait_ge(pc_v, pcv[0])
                            inst = v.reciprocal(rn_f32[0:1, :], rn_f32[0:1, :])
                            inst.then_inc(pc_v, 1); pcv[0] += 1; v.wait_ge(pc_v, pcv[0])
                            v.tensor_copy(rn_bf[0:1, :], rn_f32[0:1, :]).then_inc(p_hv, 1)
                            v.wait_ge(p_hd, B_hd + 3 * hh + 3)
                            v.wait_ge(p_he, B_he + hh + 1)
                            v.tensor_tensor(attn_fm[:, hh * 512:(hh + 1) * 512],
                                            pE[:], rn_bc[:], ALU.mult).then_inc(p_hv, 1)
                    for og in range(OGN):
                        v.wait_ge(p_wu, B_wu + 4 * (og + 1))
                        for tt in range(TPP):
                            v.wait_ge(p_xa, B_xa + 16 * (tt + 1))
                            v.tensor_tensor(out_acc[tt][:, og * 512:(og + 1) * 512],
                                            out_acc[tt][:, og * 512:(og + 1) * 512],
                                            PQ[tt][:], ALU.add).then_inc(p_oa, 1)
                    if B_dm > 0:
                        v.wait_ge(p_dm, B_dm)
                    for hc in range(ICN):
                        psg = pA if hc % 2 == 0 else pC
                        psu = pB if hc % 2 == 0 else pD
                        v.wait_ge(p_gm, B_gm + 2 * hc + 2)
                        v.wait_ge(p_sl, B_sl + hc + 1)
                        if dbg and p == 0 and hc == 0:
                            v.tensor_copy(u0[:, 0:512], psg[:])
                            v.tensor_copy(u0[:, 512:1024], psu[:])
                            v.tensor_copy(u0[:, 1024:1536], sg_buf[hc % 2][:])
                        v.tensor_tensor(h_sb[:, hc * 512:(hc + 1) * 512],
                                        sg_buf[hc % 2][:], psu[:], ALU.mult).then_inc(p_hh, 1)
                    for og in range(OGN):
                        if dbg and p == 0 and og == 0:
                            v.wait_ge(p_dbg, 32)
                        v.wait_ge(p_dm, B_dm + 4 * (og + 1))
                        for tt in range(TPP):
                            v.tensor_copy(u0[:, tt * D + og * 512:tt * D + (og + 1) * 512],
                                          PQ[tt][:]).then_inc(p_ye, 1)
                    for tt in range(TPP):
                        v.wait_ge(p_r1, B_r1 + tt + 1)
                        yjunk = dn_buf[0].ap().bitcast(F32)
                        inst = v.reduce_sum(s2_col[:, 0:1], yjunk[:, 0:D], axis=AXL.X)
                        inst.then_inc(pc_v, 1); pcv[0] += 1; v.wait_ge(pc_v, pcv[0])
                        v.tensor_copy(s2_col[0:1, 0:1], s2_col[0:1, 0:1]).then_inc(p_r2, 1)
                        v.wait_ge(p_rs, B_rs + tt + 1)
                        v.wait_ge(p_oa, B_oa + 3 * TPP + tt + 1)
                        v.scalar_tensor_tensor(out_acc[tt][:], u0[:, tt * D:(tt + 1) * D],
                                               r2_col[:, 0:1], out_acc[tt][:],
                                               ALU.mult, ALU.add).then_inc(p_fin, 1)

    return nc


def kernel(L0_attn_hidden_states, L0_attn_Wq, L0_attn_bq, L0_attn_Wk, L0_attn_bk,
           L0_attn_Wv, L0_attn_bv, L0_attn_Wo, L0_ffn_gate_up_packed,
           L0_ffn_gate_up_scales, L0_ffn_down_packed, L0_ffn_down_scales):
    x = np.ascontiguousarray(np.asarray(L0_attn_hidden_states, dtype=np.float32))
    wq_t = np.ascontiguousarray(np.asarray(L0_attn_Wq, dtype=np.float32).T)
    wk_t = np.ascontiguousarray(np.asarray(L0_attn_Wk, dtype=np.float32).T)
    wv_t = np.ascontiguousarray(np.asarray(L0_attn_Wv, dtype=np.float32).T)
    wo_t = np.ascontiguousarray(np.asarray(L0_attn_Wo, dtype=np.float32).T)
    bq = np.ascontiguousarray(np.asarray(L0_attn_bq, dtype=np.float32))
    bk = np.ascontiguousarray(np.asarray(L0_attn_bk, dtype=np.float32))
    bv = np.ascontiguousarray(np.asarray(L0_attn_bv, dtype=np.float32))
    gup = np.ascontiguousarray(np.asarray(L0_ffn_gate_up_packed, dtype=np.int32)).view(np.uint8)
    dnp = np.ascontiguousarray(np.asarray(L0_ffn_down_packed, dtype=np.int32)).view(np.uint8)
    gus = np.ascontiguousarray(np.asarray(L0_ffn_gate_up_scales, dtype=np.float16))
    dns = np.ascontiguousarray(np.asarray(L0_ffn_down_scales, dtype=np.float16))
    idb = np.eye(128, dtype=np.float32).astype(ml_dtypes.bfloat16)
    idh = np.eye(128, dtype=np.float32).astype(np.float16)

    shared = {"wq_t": wq_t, "wk_t": wk_t, "wv_t": wv_t, "wo_t": wo_t,
              "bq": bq, "bk": bk, "bv": bv, "idb": idb, "idh": idh}
    in_maps = []
    for c in range(8):
        in_maps.append({
            **shared,
            "x": np.ascontiguousarray(x[c]),
            "gup": np.ascontiguousarray(gup[c * 11 * 128:(c + 1) * 11 * 128, :]),
            "dnp": np.ascontiguousarray(dnp[c * 2 * 128:(c + 1) * 2 * 128, :]),
            "gus": np.ascontiguousarray(gus[:, c * 11 * 128:(c + 1) * 11 * 128]),
            "dns": np.ascontiguousarray(dns[:, c * 2 * 128:(c + 1) * 2 * 128]),
        })

    nc = build_program()
    res = run_bass_kernel_spmd(nc, in_maps, core_ids=list(range(8)))
    return np.stack([res.results[c]["out"] for c in range(8)], axis=0)
